# version 05
# baseline (speedup 1.0000x reference)
import math
from contextlib import ExitStack

import numpy as np
import ml_dtypes

import concourse.bass as bass
import concourse.mybir as mybir
from concourse.bass import IndirectOffsetOnAxis
from concourse.bass_utils import run_bass_kernel_spmd

F32 = mybir.dt.float32
BF16 = mybir.dt.bfloat16
FP8 = mybir.dt.float8e4
U32 = mybir.dt.uint32
AF = mybir.ActivationFunctionType
ALU = mybir.AluOpType

T = 2048
FDM = 2048
ID = 6144
NH = 16
NTC = 16
NFC = 16
NIC = 48
MSK_VAL = -10000.0 * math.sqrt(128.0)
A_BLOCKS = [0, 3, 4, 7]
B_BLOCKS = [1, 2, 5, 6]

N_WAVES = [1, 1, 2, 2]
N_CHUNKS = [2, 4, 6, 8]
N_KC = [8, 16, 24, 32]


class Ctr:
    def __init__(self, sem):
        self.sem = sem
        self.n = 0

    def inc(self, ins, k=1):
        ins.then_inc(self.sem, k)
        self.n += k
        return self.n


def build_program(stages=3, do_conv=True, cut=None):
    nc = bass.Bass("TRN2", target_bir_lowering=False)

    xin = nc.dram_tensor("x", [T, FDM], F32, kind="ExternalInput").ap()
    Wqkv = nc.dram_tensor("Wqkv", [FDM, 3 * FDM], F32, kind="ExternalInput").ap()
    bqkv_in = nc.dram_tensor("bqkv", [128, 48], F32, kind="ExternalInput").ap()
    Wo = nc.dram_tensor("Wo", [FDM, FDM], F32, kind="ExternalInput").ap()
    bo_in = nc.dram_tensor("bo", [128, 16], F32, kind="ExternalInput").ap()
    Wg = nc.dram_tensor("Wg", [FDM, ID], F32, kind="ExternalInput").ap()
    Wu = nc.dram_tensor("Wu", [FDM, ID], F32, kind="ExternalInput").ap()
    Wd = nc.dram_tensor("Wd", [ID, FDM], F32, kind="ExternalInput").ap()
    gs_in = nc.dram_tensor("gs", [128, 48], F32, kind="ExternalInput").ap()
    us_in = nc.dram_tensor("us", [128, 48], F32, kind="ExternalInput").ap()
    ds_in = nc.dram_tensor("ds", [128, 16], F32, kind="ExternalInput").ap()
    mask_in = nc.dram_tensor("mask", [T, 1024], BF16, kind="ExternalInput").ap()
    idxa_in = nc.dram_tensor("idxa", [128, 1], U32, kind="ExternalInput").ap()
    idxb_in = nc.dram_tensor("idxb", [128, 1], U32, kind="ExternalInput").ap()
    idn_in = nc.dram_tensor("idn", [128, 128], BF16, kind="ExternalInput").ap()
    idnf_in = nc.dram_tensor("idnf", [128, 128], F32, kind="ExternalInput").ap()
    onesb_in = nc.dram_tensor("onesb", [128, 128], BF16, kind="ExternalInput").ap()
    onesf_in = nc.dram_tensor("onesf", [128, 128], F32, kind="ExternalInput").ap()
    out = nc.dram_tensor("out", [T, FDM], F32, kind="ExternalOutput").ap()

    xa = nc.dram_tensor("xa", [T, FDM], F32).ap()
    xb = nc.dram_tensor("xb", [T, FDM], F32).ap()
    W8g = nc.dram_tensor("W8g", [NIC, 128, FDM], FP8).ap()
    W8u = nc.dram_tensor("W8u", [NIC, 128, FDM], FP8).ap()
    W8d = nc.dram_tensor("W8d", [NFC, 128, ID], FP8).ap()
    Qst = nc.dram_tensor("Qst", [FDM, T], BF16).ap()
    Ost = nc.dram_tensor("Ost", [FDM, T], BF16).ap()
    contribKs = [nc.dram_tensor("contribK%d" % g, [512, T], BF16).ap()
                 for g in range(4)]
    contribVs = [nc.dram_tensor("contribV%d" % g, [512, T], BF16).ap()
                 for g in range(4)]
    gatheredKs = [nc.dram_tensor("gatheredK%d" % g, [8 * 512, T], BF16).ap()
                  for g in range(4)]
    gatheredVs = [nc.dram_tensor("gatheredV%d" % g, [8 * 512, T], BF16).ap()
                  for g in range(4)]

    gctx = ExitStack()
    _uid = [0]

    def _nm(n):
        _uid[0] += 1
        return "%s_%d" % (n, _uid[0])

    sb = lambda n, s, d: gctx.enter_context(nc.sbuf_tensor(n, s, d))
    PS = gctx.enter_context(nc.psum_tensor("PS", [128, 4096], F32))
    bank = lambda i: PS[:, i * 512:(i + 1) * 512]

    idn = sb("c_idn", [128, 128], BF16)
    idnf = sb("c_idnf", [128, 128], F32)
    onesb = sb("c_onesb", [128, 128], BF16)
    onesf = sb("c_onesf", [128, 128], F32)
    c_gs = sb("c_gs", [128, 48], F32)
    c_us = sb("c_us", [128, 48], F32)
    c_dsh = sb("c_dsh", [128, 16], F32)
    c_bqkv = sb("c_bqkv", [128, 48], F32)
    c_bo = sb("c_bo", [128, 16], F32)
    c_isq = sb("c_isq", [128, 1], F32)
    c_eps = sb("c_eps", [128, 1], F32)
    c_tiny = sb("c_tiny", [128, 1], F32)
    c_half = sb("c_half", [128, 1], F32)
    c_mhalf = sb("c_mhalf", [128, 1], F32)
    c_mone = sb("c_mone", [128, 1], F32)
    c_mln128 = sb("c_mln128", [128, 1], F32)
    idxa = sb("c_idxa", [128, 1], U32)
    idxb = sb("c_idxb", [128, 1], U32)

    sPE = Ctr(gctx.enter_context(nc.semaphore("sPE")))
    sACT = Ctr(gctx.enter_context(nc.semaphore("sACT")))
    sDVE = Ctr(gctx.enter_context(nc.semaphore("sDVE")))
    sCC = Ctr(gctx.enter_context(nc.semaphore("sCC")))
    DS = {}

    def dsem(name):
        if name not in DS:
            DS[name] = Ctr(gctx.enter_context(nc.semaphore(name)))
        return DS[name]

    const_loads = [(idn, idn_in), (idnf, idnf_in), (onesb, onesb_in),
                   (onesf, onesf_in), (c_gs, gs_in), (c_us, us_in),
                   (c_dsh, ds_in), (c_bqkv, bqkv_in), (c_bo, bo_in),
                   (idxa, idxa_in), (idxb, idxb_in)]
    with nc.Block() as blk:
        @blk.sync
        def _(sync):
            for dst, src in const_loads:
                dsem("const").inc(sync.dma_start(out=dst[:], in_=src[:]), 16)

        @blk.vector
        def _(vector):
            vector.memset(c_isq[:], 1.0 / math.sqrt(128.0))
            vector.memset(c_eps[:], 1e-5)
            vector.memset(c_tiny[:], 1e-30)
            vector.memset(c_half[:], 0.5)
            vector.memset(c_mhalf[:], -0.5)
            vector.memset(c_mone[:], -1.0)
            vector.memset(c_mln128[:], -0.5 * math.log(128.0))
            vector.wait_ge(dsem("const").sem, dsem("const").n)
            sDVE.inc(vector.engine_nop())

        @blk.gpsimd
        def _(gpsimd):
            if do_conv:
                for ic in range(NIC):
                    for w_src, w_dst in [(Wg, W8g), (Wu, W8u)]:
                        src = w_src[:, ic * 128:(ic + 1) * 128].rearrange(
                            "(fc p) i -> p fc i", p=128)
                        dst = w_dst[ic].rearrange("p (fc i) -> p fc i", i=128)
                        dsem("cvt").inc(gpsimd.dma_start(out=dst, in_=src), 16)
                for hc in range(NFC):
                    for part in range(3):
                        src = Wd[part * 2048:(part + 1) * 2048,
                                 hc * 128:(hc + 1) * 128].rearrange(
                            "(ic p) h -> p ic h", p=128)
                        dst = W8d[hc].rearrange(
                            "p (ic h) -> p ic h", h=128)[
                            :, part * 16:(part + 1) * 16, :]
                        dsem("cvt").inc(gpsimd.dma_start(out=dst, in_=src), 16)

    with nc.Block() as blk:
        @blk.vector
        def _(vector):
            sDVE.inc(vector.tensor_scalar(c_dsh[:], c_dsh[:], 0.5, None,
                                          ALU.mult))

    cvt_total = dsem("cvt").n

    def emit_ln(x_src, xn_ft, lctx, src_ready=None, parts="AB", entry=None):
        lsb = lambda n, s, d: lctx.enter_context(nc.sbuf_tensor(_nm(n), s, d))
        xt = [lsb("ln_x0", [128, FDM], F32), lsb("ln_x1", [128, FDM], F32)]
        sq = lsb("ln_sq", [128, FDM], F32)
        sq2 = lsb("ln_sqb", [128, FDM], F32)
        st_s = lsb("ln_ss", [128, NTC], F32)
        st_q = lsb("ln_sq2", [128, NTC], F32)
        aa = lsb("ln_a", [128, NTC], F32)
        bb = lsb("ln_b", [128, NTC], F32)
        xnt = [lsb("ln_xn0", [128, FDM], BF16), lsb("ln_xn1", [128, FDM], BF16)]
        lnx = [dsem("lnx0"), dsem("lnx1")]

        lx0 = [lnx[0].n, lnx[1].n]
        dve0 = sDVE.n
        with nc.Block() as blk:
            @blk.vector
            def _(vector):
                for tc in range(NTC):
                    vector.wait_ge(lnx[tc % 2].sem,
                                   lx0[tc % 2] + 16 * (tc // 2 + 1))
                    if tc >= 1:
                        vector.wait_ge(sDVE.sem, dve0 + tc)
                    vector.tensor_scalar(
                        sq[:], xt[tc % 2][:], 1.0, 0.0, ALU.mult,
                        ALU.add, accum_out=st_s[:, tc:tc + 1])
                    sDVE.inc(vector.scalar_tensor_tensor(
                        sq2[:], xt[tc % 2][:], 1.0, xt[tc % 2][:],
                        ALU.mult, ALU.mult, accum_out=st_q[:, tc:tc + 1]))

            @blk.sync
            def _(sync):
                if entry is not None:
                    for c_, v_ in entry:
                        sync.wait_ge(c_.sem, v_)
                for tc in range(NTC):
                    if src_ready is not None:
                        for c_, v_ in src_ready(tc):
                            sync.wait_ge(c_.sem, v_)
                    if tc >= 2:
                        sync.wait_ge(sDVE.sem, dve0 + tc - 1)
                    lnx[tc % 2].inc(sync.dma_start(
                        out=xt[tc % 2][:],
                        in_=x_src[tc * 128:(tc + 1) * 128, :]), 16)
        assert sDVE.n == dve0 + NTC

        dveS0, actS0 = sDVE.n, sACT.n
        with nc.Block() as blk:
            @blk.vector
            def _(vector):
                d = dveS0
                sDVE.inc(vector.tensor_scalar(aa[:], st_s[:], 1.0 / FDM,
                                              None, ALU.mult)); d += 1
                vector.wait_ge(sDVE.sem, d)
                sDVE.inc(vector.scalar_tensor_tensor(
                    bb[:], aa[:], 1.0, aa[:], ALU.mult, ALU.mult)); d += 1
                vector.wait_ge(sDVE.sem, d)
                sDVE.inc(vector.tensor_scalar(st_q[:], st_q[:], 1.0 / FDM,
                                              None, ALU.mult)); d += 1
                vector.wait_ge(sDVE.sem, d)
                sDVE.inc(vector.scalar_tensor_tensor(
                    st_q[:], st_q[:], 1.0, bb[:], ALU.mult,
                    ALU.subtract)); d += 1
                vector.wait_ge(sACT.sem, actS0 + 2)
                sDVE.inc(vector.scalar_tensor_tensor(
                    bb[:], aa[:], 1.0, st_q[:], ALU.mult, ALU.mult)); d += 1
                vector.wait_ge(sDVE.sem, d)
                sDVE.inc(vector.tensor_scalar(bb[:], bb[:], -1.0, None,
                                              ALU.mult)); d += 1
                vector.wait_ge(sDVE.sem, d)
                sDVE.inc(vector.tensor_copy(aa[:], st_q[:]))

            @blk.scalar
            def _(scalar):
                scalar.wait_ge(sDVE.sem, dveS0 + 4)
                sACT.inc(scalar.activation(st_q[:], st_q[:], AF.Ln,
                                           bias=c_eps[:, 0:1], scale=1.0))
                scalar.wait_ge(sACT.sem, actS0 + 1)
                sACT.inc(scalar.activation(st_q[:], st_q[:], AF.Exp, bias=0.0,
                                           scale=c_mhalf[:, 0:1]))
        assert sDVE.n == dveS0 + 7 and sACT.n == actS0 + 2

        if parts == "A":
            return
        do_pe = parts in ("B1", "AB")
        do_evac = parts == "AB"
        lx1 = [lnx[0].n, lnx[1].n]
        dve1, act1, pe1 = sDVE.n, sACT.n, sPE.n
        with nc.Block() as blk:
            @blk.tensor
            def _(tensor):
                if not do_pe:
                    return
                for tc in range(NTC):
                    tensor.wait_ge(sDVE.sem, dve1 + tc * 9 + 1)
                    if tc >= 2 and do_evac:
                        tensor.wait_ge(sDVE.sem, dve1 + (tc - 2) * 9 + 9)
                        tensor.wait_ge(sACT.sem, act1 + (tc - 1) * 8)
                    b0 = 0 if tc % 2 == 0 else 4
                    for fc in range(NFC):
                        col = (b0 + fc // 4) * 512 + (fc % 4) * 128
                        mm = tensor.matmul(
                            PS[:, col:col + 128],
                            xnt[tc % 2][:, fc * 128:(fc + 1) * 128],
                            idn[:], start=True, stop=True)
                    sPE.inc(mm)

            @blk.scalar
            def _(scalar):
                if not do_evac:
                    return
                for tc in range(NTC):
                    scalar.wait_ge(sPE.sem, pe1 + tc + 1)
                    b0 = 0 if tc % 2 == 0 else 4
                    for fc in range(8, NFC):
                        col = (b0 + fc // 4) * 512 + (fc % 4) * 128
                        sACT.inc(scalar.activation(
                            xn_ft[:, fc * T + tc * 128: fc * T + tc * 128 + 128],
                            PS[:, col:col + 128], AF.Copy, bias=0.0, scale=1.0))

            @blk.vector
            def _(vector):
                for tc in range(NTC):
                    vector.wait_ge(lnx[tc % 2].sem,
                                   lx1[tc % 2] + 16 * (tc // 2 + 1))
                    if tc >= 2 and do_pe:
                        vector.wait_ge(sPE.sem, pe1 + tc - 1)
                    sDVE.inc(vector.tensor_scalar(
                        xnt[tc % 2][:], xt[tc % 2][:], aa[:, tc:tc + 1],
                        bb[:, tc:tc + 1], ALU.mult, ALU.add))
                    if not do_evac:
                        for fc in range(0, NFC, 2):
                            sDVE.inc(vector.engine_nop())
                        continue
                    vector.wait_ge(sPE.sem, pe1 + tc + 1)
                    b0 = 0 if tc % 2 == 0 else 4
                    for fc in range(0, 8):
                        col = (b0 + fc // 4) * 512 + (fc % 4) * 128
                        sDVE.inc(vector.tensor_copy(
                            xn_ft[:, fc * T + tc * 128: fc * T + tc * 128 + 128],
                            PS[:, col:col + 128]))

            @blk.sync
            def _(sync):
                for tc in range(NTC):
                    if src_ready is not None:
                        for c_, v_ in src_ready(tc):
                            sync.wait_ge(c_.sem, v_)
                    if tc >= 2:
                        sync.wait_ge(sDVE.sem, dve1 + (tc - 2) * 9 + 1)
                    lnx[tc % 2].inc(sync.dma_start(
                        out=xt[tc % 2][:],
                        in_=x_src[tc * 128:(tc + 1) * 128, :]), 16)
        if parts == "AB":
            assert sDVE.n == dve1 + 9 * NTC
            assert sACT.n == act1 + 8 * NTC
            assert sPE.n == pe1 + NTC

    def emit_ffn(x_src, x_dst, xn_ft, fctx, src_ready=None, first=False, entry=None):
        fsb = lambda n, s, d: fctx.enter_context(nc.sbuf_tensor(_nm(n), s, d))
        fuse = fsb("f_fuse", [128, NIC * 512], BF16)
        wg2 = [fsb("f_wg0", [128, FDM], BF16), fsb("f_wg1", [128, FDM], BF16)]
        wu2 = [fsb("f_wu0", [128, FDM], BF16), fsb("f_wu1", [128, FDM], BF16)]
        wd2 = [fsb("f_wd0", [128, ID], BF16), fsb("f_wd1", [128, ID], BF16)]
        gsil = fsb("f_gsil", [128, 512], F32)
        uu = fsb("f_uu", [128, 512], F32)
        yst = fsb("f_yst", [128, 512], F32)
        xnew = [fsb("f_xn%d" % s, [128, FDM], F32) for s in range(4)]
        swg = [dsem("wg0"), dsem("wg1")]
        swu = [dsem("wu0"), dsem("wu1")]
        swd = [dsem("wd0"), dsem("wd1")]
        sxn = [dsem("fxn%d" % s) for s in range(4)]
        sxs = [dsem("fxs%d" % s) for s in range(4)]

        for tb in range(4):
            pe0, act0, dve0 = sPE.n, sACT.n, sDVE.n
            wgb = [swg[0].n, swg[1].n]
            wub = [swu[0].n, swu[1].n]
            wdb = [swd[0].n, swd[1].n]
            xnb = [s_.n for s_ in sxn]
            xsb = [s_.n for s_ in sxs]
            pe_gu = lambda ic: pe0 + ic + 1
            pe_dn = lambda hc: pe0 + NIC + 2 * hc + 1
            pe_yt = lambda hc: pe0 + NIC + 2 * hc + 2
            act_ic = lambda ic: act0 + ic + 1
            dve_ic = lambda ic: dve0 + ic + 1
            dve_yst = lambda hc: dve0 + NIC + 2 * hc + 1
            dve_add = lambda hc: dve0 + NIC + 2 * hc + 2

            with nc.Block() as blk:
                @blk.tensor
                def _(tensor):
                    for ic in range(NIC):
                        tensor.wait_ge(swg[ic % 2].sem,
                                       wgb[ic % 2] + 16 * (ic // 2 + 1))
                        tensor.wait_ge(swu[ic % 2].sem,
                                       wub[ic % 2] + 16 * (ic // 2 + 1))
                        if ic >= 2:
                            tensor.wait_ge(sACT.sem, act_ic(ic - 2))
                        for f in range(NFC):
                            tensor.matmul(
                                bank(ic % 2),
                                wg2[ic % 2][:, f * 128:(f + 1) * 128],
                                xn_ft[:, f * T + tb * 512: f * T + tb * 512 + 512],
                                start=(f == 0), stop=(f == NFC - 1))
                        for f in range(NFC):
                            mm = tensor.matmul(
                                bank(2 + ic % 2),
                                wu2[ic % 2][:, f * 128:(f + 1) * 128],
                                xn_ft[:, f * T + tb * 512: f * T + tb * 512 + 512],
                                start=(f == 0), stop=(f == NFC - 1))
                        sPE.inc(mm)
                    for hc in range(NFC):
                        tensor.wait_ge(swd[hc % 2].sem,
                                       wdb[hc % 2] + 16 * (hc // 2 + 1))
                        if hc >= 2:
                            tensor.wait_ge(sDVE.sem, dve_yst(hc - 2))
                        for ic in range(NIC):
                            if hc == 0 and ic == 0:
                                tensor.wait_ge(sDVE.sem, dve_ic(NIC - 1))
                            mm = tensor.matmul(
                                bank(4 + hc % 2),
                                wd2[hc % 2][:, ic * 128:(ic + 1) * 128],
                                fuse[:, ic * 512:(ic + 1) * 512],
                                start=(ic == 0), stop=(ic == NIC - 1))
                        sPE.inc(mm)
                        tensor.wait_ge(sDVE.sem, dve_yst(hc))
                        if hc >= 2:
                            tensor.wait_ge(sDVE.sem, dve_add(hc - 2))
                        for s in range(4):
                            col = (6 + hc % 2) * 512 + s * 128
                            mm = tensor.matmul(PS[:, col:col + 128],
                                               yst[:, s * 128:(s + 1) * 128],
                                               idnf[:], start=True, stop=True)
                        sPE.inc(mm)

                @blk.scalar
                def _(scalar):
                    for ic in range(NIC):
                        scalar.wait_ge(sPE.sem, pe_gu(ic))
                        if ic >= 1:
                            scalar.wait_ge(sDVE.sem, dve_ic(ic - 1))
                        scalar.activation(gsil[:], bank(ic % 2), AF.Silu,
                                          bias=0.0, scale=c_gs[:, ic:ic + 1])
                        sACT.inc(scalar.activation(
                            uu[:], bank(2 + ic % 2), AF.Copy, bias=0.0,
                            scale=c_us[:, ic:ic + 1]))

                @blk.vector
                def _(vector):
                    for ic in range(NIC):
                        vector.wait_ge(sACT.sem, act_ic(ic))
                        sDVE.inc(vector.scalar_tensor_tensor(
                            fuse[:, ic * 512:(ic + 1) * 512], gsil[:], 1.0,
                            uu[:], ALU.mult, ALU.mult))
                    for hc in range(NFC):
                        vector.wait_ge(sPE.sem, pe_dn(hc))
                        sDVE.inc(vector.tensor_scalar(
                            yst[:], bank(4 + hc % 2), c_dsh[:, hc:hc + 1],
                            None, ALU.mult))
                        vector.wait_ge(sPE.sem, pe_yt(hc))
                        if hc == 0:
                            for s in range(4):
                                vector.wait_ge(sxn[s].sem, xnb[s] + 16)
                        for s in range(4):
                            col = (6 + hc % 2) * 512 + s * 128
                            vv = vector.scalar_tensor_tensor(
                                xnew[s][:, hc * 128:(hc + 1) * 128],
                                PS[:, col:col + 128], 1.0,
                                xnew[s][:, hc * 128:(hc + 1) * 128],
                                ALU.mult, ALU.add)
                        sDVE.inc(vv)

                @blk.sync
                def _(sync):
                    if tb == 0 and entry is not None:
                        for c_, v_ in entry:
                            sync.wait_ge(c_.sem, v_)
                    for s in range(4):
                        tc = tb * 4 + s
                        if src_ready is not None:
                            for c_, v_ in src_ready(tc):
                                sync.wait_ge(c_.sem, v_)
                        sync.wait_ge(sxs[s].sem, xsb[s])
                        sxn[s].inc(sync.dma_start(
                            out=xnew[s][:],
                            in_=x_src[tc * 128:(tc + 1) * 128, :]), 16)
                    sync.wait_ge(sDVE.sem, dve_add(NFC - 1))
                    for s in range(4):
                        tc = tb * 4 + s
                        sxs[s].inc(sync.dma_start(
                            out=x_dst[tc * 128:(tc + 1) * 128, :],
                            in_=xnew[s][:]), 16)

                @blk.gpsimd
                def _(gpsimd):
                    if tb == 0 and entry is not None:
                        for c_, v_ in entry:
                            gpsimd.wait_ge(c_.sem, v_)
                    for ic in range(NIC):
                        if first and tb == 0 and ic < 2:
                            gpsimd.wait_ge(dsem("cvt").sem, cvt_total)
                        if ic >= 2:
                            gpsimd.wait_ge(sPE.sem, pe_gu(ic - 2))
                        swg[ic % 2].inc(gpsimd.dma_start(out=wg2[ic % 2][:],
                                                         in_=W8g[ic]), 16)
                        swu[ic % 2].inc(gpsimd.dma_start(out=wu2[ic % 2][:],
                                                         in_=W8u[ic]), 16)
                    for hc in range(NFC):
                        if hc >= 2:
                            gpsimd.wait_ge(sPE.sem, pe_dn(hc - 2))
                        swd[hc % 2].inc(gpsimd.dma_start(out=wd2[hc % 2][:],
                                                         in_=W8d[hc]), 16)

            assert sPE.n == pe0 + NIC + 2 * NFC
            assert sACT.n == act0 + NIC
            assert sDVE.n == dve0 + NIC + 2 * NFC
        return sxs

    def emit_qkv(xn_ft, qctx):
        qsb = lambda n, s, d: qctx.enter_context(nc.sbuf_tensor(_nm(n), s, d))
        wq2 = [qsb("q_w0", [128, FDM], BF16), qsb("q_w1", [128, FDM], BF16)]
        qkv2 = [qsb("q_s0", [128, FDM], BF16), qsb("q_s1", [128, FDM], BF16)]
        vstage = [qsb("q_v0", [128, FDM], BF16), qsb("q_v1", [128, FDM], BF16)]
        swq = [dsem("qw0"), dsem("qw1")]
        sqst = [dsem("qst0"), dsem("qst1")]
        svst = [dsem("vst0"), dsem("vst1")]

        order = []
        for g in range(4):
            order += list(range(16 + 4 * g, 16 + 4 * g + 4))
            order += list(range(32 + 4 * g, 32 + 4 * g + 4))
            order += list(range(4 * g, 4 * g + 4))
        pe0, act0, dve0 = sPE.n, sACT.n, sDVE.n
        wqb = [swq[0].n, swq[1].n]
        qstb = [sqst[0].n, sqst[1].n]
        vstb = [svst[0].n, svst[1].n]

        pe_sched = []
        acc = pe0
        for j, qc in enumerate(order):
            acc += 1
            mm_inc = acc
            t_inc = None
            if qc >= 32:
                acc += 1
                t_inc = acc
            pe_sched.append((mm_inc, t_inc))
        act_j = lambda j: act0 + j + 1
        dve_sched = []
        acc = dve0
        for j, qc in enumerate(order):
            if qc >= 32:
                acc += 1
                dve_sched.append(acc)
            else:
                dve_sched.append(None)
        qst_cnt = [0, 0]
        vst_cnt = [0, 0]
        st_at = {}
        for j, qc in enumerate(order):
            p = j % 2
            if qc >= 32:
                vst_cnt[p] += 16
                st_at[j] = (svst[p], vstb[p] + vst_cnt[p])
            else:
                qst_cnt[p] += 16
                st_at[j] = (sqst[p], qstb[p] + qst_cnt[p])

        with nc.Block() as blk:
            @blk.tensor
            def _(tensor):
                for j, qc in enumerate(order):
                    base = 0 if j % 2 == 0 else 2048
                    tensor.wait_ge(swq[j % 2].sem,
                                   wqb[j % 2] + 16 * (j // 2 + 1))
                    if j >= 2:
                        tensor.wait_ge(sACT.sem, act_j(j - 2))
                        if order[j - 2] >= 32:
                            tensor.wait_ge(sDVE.sem, dve_sched[j - 2])
                    for f in range(NFC):
                        for tq in range(4):
                            mm = tensor.matmul(
                                PS[:, base + tq * 512: base + tq * 512 + 512],
                                wq2[j % 2][:, f * 128:(f + 1) * 128],
                                xn_ft[:, f * T + tq * 512: f * T + tq * 512 + 512],
                                start=(f == 0), stop=(f == NFC - 1))
                    sPE.inc(mm)
                    if qc >= 32:
                        tensor.wait_ge(sACT.sem, act_j(j))
                        for tcg in range(16):
                            col = base + (tcg // 4) * 512 + (tcg % 4) * 128
                            mm = tensor.matmul(
                                PS[:, col:col + 128],
                                qkv2[j % 2][:, tcg * 128:(tcg + 1) * 128],
                                idn[:], start=True, stop=True)
                        sPE.inc(mm)

            @blk.scalar
            def _(scalar):
                for j, qc in enumerate(order):
                    base = 0 if j % 2 == 0 else 2048
                    scalar.wait_ge(sPE.sem, pe_sched[j][0])
                    if j >= 2:
                        c_, v_ = st_at[j - 2]
                        scalar.wait_ge(c_.sem, v_)
                    sACT.inc(scalar.activation(
                        qkv2[j % 2][:], PS[:, base:base + 2048], AF.Identity,
                        bias=c_bqkv[:, qc:qc + 1], scale=1.0))

            @blk.vector
            def _(vector):
                for j, qc in enumerate(order):
                    if qc < 32:
                        continue
                    base = 0 if j % 2 == 0 else 2048
                    vector.wait_ge(sPE.sem, pe_sched[j][1])
                    if j >= 2 and order[j - 2] >= 32:
                        c_, v_ = st_at[j - 2]
                        vector.wait_ge(c_.sem, v_)
                    for g in range(4):
                        vv = vector.tensor_copy(
                            vstage[j % 2][:, g * 512:(g + 1) * 512],
                            PS[:, base + g * 512: base + g * 512 + 512])
                    sDVE.inc(vv)

            @blk.sync
            def _(sync):
                for j, qc in enumerate(order):
                    head = qc % 16
                    hg, hr = head // 4, head % 4
                    if qc >= 32:
                        sync.wait_ge(sDVE.sem, dve_sched[j])
                        st_at[j][0].inc(sync.dma_start(
                            out=contribVs[hg][hr * 128:(hr + 1) * 128, :],
                            in_=vstage[j % 2][:]), 16)
                    elif qc >= 16:
                        sync.wait_ge(sACT.sem, act_j(j))
                        st_at[j][0].inc(sync.dma_start(
                            out=contribKs[hg][hr * 128:(hr + 1) * 128, :],
                            in_=qkv2[j % 2][:]), 16)
                    else:
                        sync.wait_ge(sACT.sem, act_j(j))
                        st_at[j][0].inc(sync.dma_start(
                            out=Qst[head * 128:(head + 1) * 128, :],
                            in_=qkv2[j % 2][:]), 16)

            @blk.gpsimd
            def _(gpsimd):
                for j, qc in enumerate(order):
                    if j >= 2:
                        gpsimd.wait_ge(sPE.sem, pe_sched[j - 2][0])
                    src = Wqkv[:, qc * 128:(qc + 1) * 128].rearrange(
                        "(fc p) q -> p fc q", p=128)
                    dst = wq2[j % 2][:].rearrange("p (fc q) -> p fc q", q=128)
                    swq[j % 2].inc(gpsimd.dma_start(out=dst, in_=src), 16)
                    if j % 12 == 7:
                        g = j // 12
                        gpsimd.wait_ge(sqst[0].sem, qstb[0] + 64 * g + 32)
                        gpsimd.wait_ge(sqst[1].sem, qstb[1] + 64 * g + 32)
                        sCC.inc(gpsimd.collective_compute(
                            "AllGather", ALU.bypass,
                            replica_groups=[list(range(8))],
                            ins=[contribKs[g].opt()],
                            outs=[gatheredKs[g].opt()]), 1)
                    if j % 12 == 11:
                        g = j // 12
                        gpsimd.wait_ge(svst[0].sem, vstb[0] + 32 * (g + 1))
                        gpsimd.wait_ge(svst[1].sem, vstb[1] + 32 * (g + 1))
                        sCC.inc(gpsimd.collective_compute(
                            "AllGather", ALU.bypass,
                            replica_groups=[list(range(8))],
                            ins=[contribVs[g].opt()],
                            outs=[gatheredVs[g].opt()]), 1)

        assert sACT.n == act0 + 48
        q_store_at = {order[j]: st_at[j] for j in range(48) if order[j] < 16}
        qkv_store_totals = [(sqst[0], sqst[0].n), (sqst[1], sqst[1].n),
                            (svst[0], svst[0].n), (svst[1], svst[1].n)]
        return q_store_at, qkv_store_totals

    def emit_attn(actx, dmabufs, q_store_at, qkv_store_totals, entry=None):
        asb = lambda n, s, d: actx.enter_context(nc.sbuf_tensor(_nm(n), s, d))
        Ksb, Vsb, Qsb, msk = dmabufs
        Psb = [asb("a_p%d" % q, [128, 4096], BF16) for q in range(4)]
        PTs = [asb("a_pt%d" % i, [128, 512], BF16) for i in range(4)]
        Bsb = asb("a_B", [128, 512], F32)
        Ohd = [asb("a_o0", [128, FDM], BF16), asb("a_o1", [128, FDM], BF16)]
        K2 = asb("a_k2", [128, 4096], BF16)
        Q2 = asb("a_q2", [128, FDM], BF16)
        mxc = asb("a_mx", [128, 8], F32)
        nrm = asb("a_nrm", [128, 16], F32)
        negm = asb("a_negm", [128, 16], F32)
        sgp = asb("a_sgp", [128, 64], F32)
        sg = asb("a_sg", [128, 4], F32)
        rsgT = asb("a_rsgT", [1, 512], F32)
        skk = [dsem("kk0"), dsem("kk1")]
        svv = [dsem("vv0"), dsem("vv1")]
        sqq = [dsem("qq0"), dsem("qq1")]
        soh = [dsem("oh0"), dsem("oh1")]
        smk = dsem("mskS")

        with nc.Block() as blk:
            @blk.sync
            def _(sync):
                if entry is not None:
                    for c_, v_ in entry:
                        sync.wait_ge(c_.sem, v_)
                for s in range(4):
                    src = mask_in[s * 512:(s + 1) * 512, :].rearrange(
                        "(qs p) k -> p qs k", p=128)
                    dst = msk[s][:].rearrange("p (qs k) -> p qs k", k=1024)
                    smk.inc(sync.dma_start(out=dst, in_=src), 16)
        mask_total = smk.n

        kb0 = [skk[0].n, skk[1].n]
        vb0 = [svv[0].n, svv[1].n]
        qb0 = [sqq[0].n, sqq[1].n]
        ob0 = [soh[0].n, soh[1].n]

        HS = []
        pe, act, dve = sPE.n, sACT.n, sDVE.n
        for h in range(NH):
            peS, actS, dveS = {}, {}, {}
            pe += 1; peS["mxw0"] = pe
            pe += 1; peS["mxw1"] = pe
            pe += 1; peS["q2"] = pe
            dve += 1; dveS["k2q2"] = dve
            dve += 1; dveS["mxr0"] = dve
            dve += 1; dveS["mxr1"] = dve
            dve += 1; dveS["mxf"] = dve
            dve += 1; dveS["tprod"] = dve
            act += 2; actS["mest"] = act
            dve += 1; dveS["negm"] = dve
            wave_keys = []
            for s in range(4):
                for q in range(4):
                    for wi in range(s + 1):
                        pe += 1; peS[("S", s, q, wi)] = pe
                        act += 1; actS[("exp", s, q, wi)] = act
                        wave_keys.append((s, q, wi))
                dve += 1; dveS[("sig", s)] = dve
                act += 2; actS[("rsg", s)] = act
                for kc in range(N_KC[s]):
                    pe += 1; peS[("T", s, kc)] = pe
                    if kc % 4 < 3:
                        dve += 1; dveS[("PTe", s, kc)] = dve
                    else:
                        act += 1; actS[("PTe", s, kc)] = act
                pe += 1; peS[("rsgT", s)] = pe
                dve += 1; dveS[("rsgTe", s)] = dve
                pe += 1; peS[("Bmm", s)] = pe
                dve += 1; dveS[("Be", s)] = dve
                pe += 1; peS[("AV", s)] = pe
                dve += 1; dveS[("Oe", s)] = dve
            HS.append((peS, actS, dveS, wave_keys))

        guard = {b: None for b in range(8)}

        def bank_waits(eng, banks):
            done = set()
            for b in banks:
                g = guard[b]
                if g is not None and g not in done:
                    eng.wait_ge(g[0], g[1])
                    done.add(g)
                guard[b] = None

        with nc.Block() as blk:
            @blk.tensor
            def _(tensor):
                for h in range(NH):
                    peS, actS, dveS, wave_keys = HS[h]
                    tensor.wait_ge(sDVE.sem, dveS["k2q2"])
                    for w in range(2):
                        if w == 1:
                            tensor.wait_ge(sDVE.sem, dveS["mxr0"])
                        bank_waits(tensor, [0, 1, 2, 3])
                        for g in range(4):
                            mm = tensor.matmul(bank(g), onesb[:],
                                               K2[:, (w * 4 + g) * 512:
                                                  (w * 4 + g + 1) * 512],
                                               start=True, stop=True)
                        sPE.inc(mm)
                        for g in range(4):
                            guard[g] = (sDVE.sem, dveS["mxr%d" % w])
                    tensor.wait_ge(sDVE.sem, dveS["mxr1"])
                    bank_waits(tensor, [7])
                    for qs in range(16):
                        mm = tensor.matmul(
                            PS[:, 7 * 512 + qs: 7 * 512 + qs + 1],
                            Q2[:, qs * 128:(qs + 1) * 128],
                            onesb[:, 0:1], start=True, stop=True)
                    sPE.inc(mm)
                    guard[7] = (sDVE.sem, dveS["tprod"])
                    widx = 0
                    for s in range(4):
                        for q in range(4):
                            qt = s * 4 + q
                            for wi in range(s + 1):
                                bp = widx % 4
                                tensor.wait_ge(sDVE.sem, dveS["negm"])
                                if h == 0 and widx == 0:
                                    tensor.wait_ge(smk.sem, mask_total)
                                if widx >= 3:
                                    tensor.wait_ge(
                                        sACT.sem,
                                        actS[("exp",) + wave_keys[widx - 3]])
                                bank_waits(tensor, [2 * bp, 2 * bp + 1])
                                for ci in range(2):
                                    c = wi * 2 + ci
                                    if c <= s:
                                        kap = Ksb[h % 2][:, c * 512:
                                                         (c + 1) * 512]
                                    else:
                                        cb = c - s - 1
                                        kap = Ksb[h % 2][:, 2048 + cb * 512:
                                                         2048 + cb * 512 + 512]
                                    has_mask = (c == s or c == 2 * s + 1)
                                    mm = tensor.matmul(
                                        PS[:, (2 * bp + ci) * 512:
                                           (2 * bp + ci) * 512 + 512],
                                        Qsb[h % 2][:, qt * 128:(qt + 1) * 128],
                                        kap, start=True, stop=not has_mask)
                                    if has_mask:
                                        moff = 0 if c == s else 512
                                        mm = tensor.matmul(
                                            PS[:, (2 * bp + ci) * 512:
                                               (2 * bp + ci) * 512 + 512],
                                            idn[:],
                                            msk[s][:, q * 1024 + moff:
                                                   q * 1024 + moff + 512],
                                            start=False, stop=True)
                                sPE.inc(mm)
                                gv = (sACT.sem, actS[("exp", s, q, wi)])
                                guard[2 * bp] = gv
                                guard[2 * bp + 1] = gv
                                widx += 1
                        nkc = N_KC[s]
                        na = 4 * (s + 1)

                        def emit_av(kc, nkc=nkc, na=na, s=s, h=h,
                                    dveS=dveS, actS=actS):
                            who = ("PTe", s, kc)
                            if kc % 4 < 3:
                                tensor.wait_ge(sDVE.sem, dveS[who])
                            else:
                                tensor.wait_ge(sACT.sem, actS[who])
                            if kc == 0:
                                bank_waits(tensor, [6])
                            if kc < na:
                                vap = Vsb[h % 2][:, kc * 128:(kc + 1) * 128]
                            else:
                                vap = Vsb[h % 2][:, 2048 + (kc - na) * 128:
                                                 2048 + (kc - na) * 128 + 128]
                            return tensor.matmul(bank(6), vap,
                                                 PTs[kc % 4][:],
                                                 start=(kc == 0),
                                                 stop=(kc == nkc - 1))

                        for kc in range(nkc):
                            if kc == 0:
                                tensor.wait_ge(sACT.sem,
                                               actS[("exp", s, 3, s)])
                            bnk = kc % 4
                            bank_waits(tensor, [bnk])
                            for q in range(4):
                                col = bnk * 512 + q * 128
                                mm = tensor.matmul(
                                    PS[:, col:col + 128],
                                    Psb[q][:, kc * 128:(kc + 1) * 128],
                                    idn[:], start=True, stop=True)
                            sPE.inc(mm)
                            who = ("PTe", s, kc)
                            guard[bnk] = ((sDVE.sem, dveS[who])
                                          if kc % 4 < 3
                                          else (sACT.sem, actS[who]))
                            if kc >= 2:
                                mm = emit_av(kc - 2)
                        tensor.wait_ge(sDVE.sem, dveS[("sig", s)])
                        tensor.wait_ge(sACT.sem, actS[("rsg", s)])
                        bank_waits(tensor, [7])
                        for q in range(4):
                            mm = tensor.matmul(
                                PS[0:1, 7 * 512 + q * 128:
                                   7 * 512 + q * 128 + 128],
                                sg[:, q:q + 1], idnf[:],
                                start=True, stop=True)
                        sPE.inc(mm)
                        guard[7] = (sDVE.sem, dveS[("rsgTe", s)])
                        tensor.wait_ge(sDVE.sem, dveS[("rsgTe", s)])
                        bank_waits(tensor, [7])
                        for q in range(4):
                            mm = tensor.matmul(
                                PS[:, 7 * 512 + q * 128:
                                   7 * 512 + (q + 1) * 128],
                                onesf[0:1, :],
                                rsgT[0:1, q * 128:(q + 1) * 128],
                                start=True, stop=True)
                        sPE.inc(mm)
                        guard[7] = (sDVE.sem, dveS[("Be", s)])
                        mm = emit_av(nkc - 2)
                        mm = emit_av(nkc - 1)
                        sPE.inc(mm)
                        guard[6] = (sDVE.sem, dveS[("Oe", s)])

            @blk.scalar
            def _(scalar):
                for h in range(NH):
                    peS, actS, dveS, wave_keys = HS[h]
                    scalar.wait_ge(sDVE.sem, dveS["tprod"])
                    sACT.inc(scalar.activation(nrm[:], nrm[:], AF.Ln,
                                               bias=c_tiny[:, 0:1], scale=1.0))
                    scalar.wait_ge(sACT.sem, actS["mest"] - 1)
                    sACT.inc(scalar.activation(nrm[:], nrm[:], AF.Exp,
                                               bias=c_mln128[:, 0:1],
                                               scale=c_half[:, 0:1]))
                    widx = 0
                    for s in range(4):
                        for q in range(4):
                            qt = s * 4 + q
                            for wi in range(s + 1):
                                bp = widx % 4
                                scalar.wait_ge(sPE.sem, peS[("S", s, q, wi)])
                                if h >= 1 and widx == 0:
                                    scalar.wait_ge(
                                        sPE.sem,
                                        HS[h - 1][0][("T", 3, N_KC[3] - 1)])
                                sACT.inc(scalar.activation(
                                    Psb[q][:, wi * 1024: wi * 1024 + 1024],
                                    PS[:, bp * 1024: bp * 1024 + 1024],
                                    AF.Exp, bias=negm[:, qt:qt + 1],
                                    scale=c_isq[:, 0:1],
                                    accum_out=sgp[:, qt * 4 + wi:
                                                  qt * 4 + wi + 1]))
                                widx += 1
                        scalar.wait_ge(sDVE.sem, dveS[("sig", s)])
                        sACT.inc(scalar.activation(sg[:], sg[:], AF.Ln,
                                                   bias=c_tiny[:, 0:1],
                                                   scale=1.0))
                        scalar.wait_ge(sACT.sem, actS[("rsg", s)] - 1)
                        sACT.inc(scalar.activation(sg[:], sg[:], AF.Exp,
                                                   bias=0.0,
                                                   scale=c_mone[:, 0:1]))
                        for kc in range(N_KC[s]):
                            if kc % 4 < 3:
                                continue
                            scalar.wait_ge(sPE.sem, peS[("T", s, kc)])
                            sACT.inc(scalar.activation(
                                PTs[kc % 4][:],
                                PS[:, (kc % 4) * 512:(kc % 4) * 512 + 512],
                                AF.Copy, bias=0.0, scale=1.0))

            @blk.vector
            def _(vector):
                for h in range(NH):
                    peS, actS, dveS, wave_keys = HS[h]
                    kv_val = 16 * 2 * (h // 2 + 1)
                    vector.wait_ge(skk[h % 2].sem, kb0[h % 2] + kv_val)
                    vector.wait_ge(svv[h % 2].sem, vb0[h % 2] + kv_val)
                    vector.wait_ge(sqq[h % 2].sem,
                                   qb0[h % 2] + 16 * (h // 2 + 1))
                    vector.scalar_tensor_tensor(
                        K2[:], Ksb[h % 2][:], 1.0, Ksb[h % 2][:],
                        ALU.mult, ALU.mult)
                    sDVE.inc(vector.scalar_tensor_tensor(
                        Q2[:], Qsb[h % 2][:], 1.0, Qsb[h % 2][:],
                        ALU.mult, ALU.mult))
                    for w in range(2):
                        vector.wait_ge(sPE.sem, peS["mxw%d" % w])
                        for g in range(4):
                            vv = vector.reduce_max(
                                mxc[:, w * 4 + g:w * 4 + g + 1],
                                bank(g), mybir.AxisListType.X)
                        sDVE.inc(vv)
                    vector.wait_ge(sDVE.sem, dveS["mxr1"])
                    sDVE.inc(vector.reduce_max(mxc[:, 0:1], mxc[:, 0:8],
                                               mybir.AxisListType.X))
                    vector.wait_ge(sDVE.sem, dveS["mxf"])
                    vector.wait_ge(sPE.sem, peS["q2"])
                    sDVE.inc(vector.tensor_scalar(
                        nrm[:], PS[:, 7 * 512: 7 * 512 + 16], mxc[:, 0:1],
                        None, ALU.mult))
                    vector.wait_ge(sACT.sem, actS["mest"])
                    sDVE.inc(vector.tensor_scalar(
                        negm[:], nrm[:], -1.0, None, ALU.mult))
                    for s in range(4):
                        nw = s + 1
                        vector.wait_ge(sACT.sem, actS[("exp", s, 3, nw - 1)])
                        for q in range(4):
                            qt = s * 4 + q
                            if nw == 1:
                                vv = vector.tensor_copy(
                                    sg[:, q:q + 1],
                                    sgp[:, qt * 4:qt * 4 + 1])
                            else:
                                vv = vector.reduce_sum(
                                    sg[:, q:q + 1],
                                    sgp[:, qt * 4:qt * 4 + nw],
                                    mybir.AxisListType.X)
                        sDVE.inc(vv)
                        for kc in range(N_KC[s]):
                            if kc % 4 >= 3:
                                continue
                            vector.wait_ge(sPE.sem, peS[("T", s, kc)])
                            sDVE.inc(vector.tensor_copy(
                                PTs[kc % 4][:],
                                PS[:, (kc % 4) * 512:(kc % 4) * 512 + 512]))
                        vector.wait_ge(sPE.sem, peS[("rsgT", s)])
                        sDVE.inc(vector.tensor_copy(
                            rsgT[0:1, :], PS[0:1, 7 * 512: 7 * 512 + 512]))
                        vector.wait_ge(sPE.sem, peS[("Bmm", s)])
                        sDVE.inc(vector.tensor_copy(
                            Bsb[:], PS[:, 7 * 512: 7 * 512 + 512]))
                        vector.wait_ge(sPE.sem, peS[("AV", s)])
                        vector.wait_ge(sDVE.sem, dveS[("Be", s)])
                        if h >= 2 and s == 0:
                            vector.wait_ge(soh[h % 2].sem,
                                           ob0[h % 2] + 16 * (h // 2))
                        sDVE.inc(vector.scalar_tensor_tensor(
                            Ohd[h % 2][:, s * 512:(s + 1) * 512],
                            bank(6), 1.0, Bsb[:], ALU.mult, ALU.mult))

            @blk.sync
            def _(sync):
                for h in range(NH):
                    peS, actS, dveS, wave_keys = HS[h]
                    if h >= 2:
                        sync.wait_ge(sPE.sem, HS[h - 2][0][("AV", 3)])
                    sync.wait_ge(q_store_at[h][0].sem, q_store_at[h][1])
                    sqq[h % 2].inc(sync.dma_start(
                        out=Qsb[h % 2][:],
                        in_=Qst[h * 128:(h + 1) * 128, :]), 16)
                    sync.wait_ge(sDVE.sem, dveS[("Oe", 3)])
                    soh[h % 2].inc(sync.dma_start(
                        out=Ost[h * 128:(h + 1) * 128, :],
                        in_=Ohd[h % 2][:]), 16)

            @blk.gpsimd
            def _(gpsimd):
                for h in range(NH):
                    hg, hr = h // 4, h % 4
                    if hr == 0:
                        gpsimd.wait_ge(sCC.sem, 2 * hg + 1)
                    if h >= 2:
                        gpsimd.wait_ge(sPE.sem, HS[h - 2][0][("AV", 3)])
                    for rank, idx in [(0, idxa), (1, idxb)]:
                        skk[h % 2].inc(gpsimd.indirect_dma_start(
                            out=Ksb[h % 2][:, rank * 2048: rank * 2048 + 2048],
                            out_offset=None, in_=gatheredKs[hg],
                            in_offset=IndirectOffsetOnAxis(ap=idx[:, 0:1],
                                                           axis=0),
                            element_offset=(hr * 128) * T), 16)
                    if hr == 0:
                        gpsimd.wait_ge(sCC.sem, 2 * hg + 2)
                    for rank, idx in [(0, idxa), (1, idxb)]:
                        svv[h % 2].inc(gpsimd.indirect_dma_start(
                            out=Vsb[h % 2][:, rank * 2048: rank * 2048 + 2048],
                            out_offset=None, in_=gatheredVs[hg],
                            in_offset=IndirectOffsetOnAxis(ap=idx[:, 0:1],
                                                           axis=0),
                            element_offset=(hr * 128) * T), 16)

        assert sPE.n == pe and sACT.n == act and sDVE.n == dve
        oh_totals = [(soh[0], soh[0].n), (soh[1], soh[1].n)]
        return oh_totals

    def emit_outproj(x_src, x_dst, octx, oh_totals, src_ready):
        osb = lambda n, s, d: octx.enter_context(nc.sbuf_tensor(_nm(n), s, d))
        wo2 = [osb("o_w0", [128, FDM], BF16), osb("o_w1", [128, FDM], BF16)]
        Ot = [osb("o_t0", [128, 16 * 512], BF16),
              osb("o_t1", [128, 16 * 512], BF16)]
        yst = osb("o_yst", [128, 512], F32)
        xnew = [osb("o_xn%d" % s, [128, FDM], F32) for s in range(4)]
        swo = [dsem("ow0"), dsem("ow1")]
        sot = [dsem("oto0"), dsem("oto1")]
        sxn = [dsem("fxn%d" % s) for s in range(4)]
        sxs = [dsem("fxs%d" % s) for s in range(4)]

        for tb in range(4):
            pe0, act0, dve0 = sPE.n, sACT.n, sDVE.n
            wob = [swo[0].n, swo[1].n]
            otb = [sot[0].n, sot[1].n]
            xnb = [s_.n for s_ in sxn]
            xsb = [s_.n for s_ in sxs]
            pe_mm = lambda hc: pe0 + 2 * hc + 1
            pe_yt = lambda hc: pe0 + 2 * hc + 2
            dve_y = lambda hc: dve0 + 2 * hc + 1
            dve_a = lambda hc: dve0 + 2 * hc + 2

            with nc.Block() as blk:
                @blk.tensor
                def _(tensor):
                    for hc in range(NFC):
                        tensor.wait_ge(swo[hc % 2].sem,
                                       wob[hc % 2] + 16 * (hc // 2 + 1))
                        if hc == 0:
                            tensor.wait_ge(sot[tb % 2].sem, otb[tb % 2] + 16)
                        if hc >= 2:
                            tensor.wait_ge(sDVE.sem, dve_y(hc - 2))
                        for oc in range(NFC):
                            mm = tensor.matmul(
                                bank(hc % 2),
                                wo2[hc % 2][:, oc * 128:(oc + 1) * 128],
                                Ot[tb % 2][:, oc * 512:(oc + 1) * 512],
                                start=(oc == 0), stop=(oc == NFC - 1))
                        sPE.inc(mm)
                        tensor.wait_ge(sDVE.sem, dve_y(hc))
                        if hc >= 2:
                            tensor.wait_ge(sDVE.sem, dve_a(hc - 2))
                        for s in range(4):
                            col = (2 + hc % 2) * 512 + s * 128
                            mm = tensor.matmul(PS[:, col:col + 128],
                                               yst[:, s * 128:(s + 1) * 128],
                                               idnf[:], start=True, stop=True)
                        sPE.inc(mm)

                @blk.vector
                def _(vector):
                    for hc in range(NFC):
                        vector.wait_ge(sPE.sem, pe_mm(hc))
                        if hc >= 1:
                            vector.wait_ge(sPE.sem, pe_yt(hc - 1))
                        sDVE.inc(vector.tensor_scalar(
                            yst[:], bank(hc % 2), c_bo[:, hc:hc + 1],
                            None, ALU.add))
                        vector.wait_ge(sPE.sem, pe_yt(hc))
                        if hc == 0:
                            for s in range(4):
                                vector.wait_ge(sxn[s].sem, xnb[s] + 16)
                        for s in range(4):
                            col = (2 + hc % 2) * 512 + s * 128
                            vv = vector.scalar_tensor_tensor(
                                xnew[s][:, hc * 128:(hc + 1) * 128],
                                PS[:, col:col + 128], 1.0,
                                xnew[s][:, hc * 128:(hc + 1) * 128],
                                ALU.mult, ALU.add)
                        sDVE.inc(vv)

                @blk.sync
                def _(sync):
                    if tb < 2:
                        for c_, v_ in oh_totals:
                            sync.wait_ge(c_.sem, v_)
                    sync.wait_ge(sot[tb % 2].sem, otb[tb % 2])
                    sot[tb % 2].inc(sync.dma_start(
                        out=Ot[tb % 2][:],
                        in_=Ost[:, tb * 512:(tb + 1) * 512].rearrange(
                            "(oc p) t -> p oc t", p=128)), 16)
                    for s in range(4):
                        tc = tb * 4 + s
                        for c_, v_ in src_ready(tc):
                            sync.wait_ge(c_.sem, v_)
                        sync.wait_ge(sxs[s].sem, xsb[s])
                        sxn[s].inc(sync.dma_start(
                            out=xnew[s][:],
                            in_=x_src[tc * 128:(tc + 1) * 128, :]), 16)
                    sync.wait_ge(sDVE.sem, dve_a(NFC - 1))
                    for s in range(4):
                        tc = tb * 4 + s
                        sxs[s].inc(sync.dma_start(
                            out=x_dst[tc * 128:(tc + 1) * 128, :],
                            in_=xnew[s][:]), 16)

                @blk.gpsimd
                def _(gpsimd):
                    for hc in range(NFC):
                        if hc >= 2:
                            gpsimd.wait_ge(sPE.sem, pe_mm(hc - 2))
                        src = Wo[:, hc * 128:(hc + 1) * 128].rearrange(
                            "(oc p) h -> p oc h", p=128)
                        dst = wo2[hc % 2][:].rearrange("p (oc h) -> p oc h",
                                                       h=128)
                        swo[hc % 2].inc(gpsimd.dma_start(out=dst, in_=src), 16)

            assert sPE.n == pe0 + 2 * NFC
            assert sDVE.n == dve0 + 2 * NFC
        return sxs

    def store_ready(sxs_list, base_counts):
        def f(tc):
            s = tc % 4
            k = tc // 4
            return [(sxs_list[s], base_counts[s] + 16 * (k + 1))]
        return f

    def sublayer_ffn(x_src, x_dst, src_ready=None, first=False):
        entry = (None if src_ready is None else
                 [src_ready(12 + i)[0] for i in range(4)])
        with ExitStack() as fctx:
            xn_ft = fctx.enter_context(
                nc.sbuf_tensor(_nm("xn_ft"), [128, NFC * T], BF16))
            with ExitStack() as lctx:
                emit_ln(x_src, xn_ft, lctx, src_ready, entry=entry)
            base = [dsem("fxs%d" % s).n for s in range(4)]
            sxs = emit_ffn(x_src, x_dst, xn_ft, fctx, src_ready, first,
                           entry=entry)
        return store_ready(sxs, base)

    def sublayer_attn(x_src, x_dst, src_ready):
        with ExitStack() as octx0:
            ksb_ = [octx0.enter_context(nc.sbuf_tensor(_nm("a_k%d" % i),
                                                       [128, 4096], BF16))
                    for i in range(2)]
            vsb_ = [octx0.enter_context(nc.sbuf_tensor(_nm("a_v%d" % i),
                                                       [128, 4096], BF16))
                    for i in range(2)]
            qsb_ = [octx0.enter_context(nc.sbuf_tensor(_nm("a_q%d" % i),
                                                       [128, FDM], BF16))
                    for i in range(2)]
            msk_ = [octx0.enter_context(nc.sbuf_tensor(_nm("a_m%d" % i),
                                                       [128, 4096], BF16))
                    for i in range(4)]
            entry = (None if src_ready is None else
                     [src_ready(12 + i)[0] for i in range(4)])
            with ExitStack() as qctx:
                xn_ft = qctx.enter_context(
                    nc.sbuf_tensor(_nm("xn_ft"), [128, NFC * T], BF16))
                with ExitStack() as lctx:
                    emit_ln(x_src, xn_ft, lctx, src_ready, entry=entry)
                q_store_at, qkv_totals = emit_qkv(xn_ft, qctx)
            with ExitStack() as actx:
                oh_totals = emit_attn(actx, (ksb_, vsb_, qsb_, msk_),
                                      q_store_at, qkv_totals, entry=entry)
        with ExitStack() as octx:
            base = [dsem("fxs%d" % s).n for s in range(4)]
            sxs = emit_outproj(x_src, x_dst, octx, oh_totals, src_ready)
        return store_ready(sxs, base)

    if cut is not None and cut.startswith("ln"):
        parts = cut[2:] if len(cut) > 2 else "AB"
        with ExitStack() as fctx:
            xn_ft = fctx.enter_context(
                nc.sbuf_tensor(_nm("xn_ft"), [128, NFC * T], BF16))
            with ExitStack() as lctx:
                emit_ln(xin, xn_ft, lctx, None, parts)
    elif stages == 1:
        sublayer_ffn(xin, out, None, first=True)
    elif stages == 2:
        r1 = sublayer_ffn(xin, xa, None, first=True)
        sublayer_attn(xa, out, r1)
    else:
        r1 = sublayer_ffn(xin, xa, None, first=True)
        r2 = sublayer_attn(xa, xb, r1)
        sublayer_ffn(xb, out, r2, first=False)

    with nc.Block() as blk:
        @blk.sync
        def _(sync):
            for nm, c in DS.items():
                sync.wait_ge(c.sem, c.n)

    gctx.close()
    return nc


_CACHE = {}


def _get_nc(stages):
    if stages not in _CACHE:
        _CACHE[stages] = build_program(stages)
    return _CACHE[stages]


def make_core_inputs(inputs):
    x = np.asarray(inputs["L0_attn_x"])
    blocks = {0: A_BLOCKS, 1: B_BLOCKS}

    base = {
        "Wqkv": np.ascontiguousarray(np.asarray(inputs["L0_attn_Wqkv"])),
        "bqkv": np.ascontiguousarray(
            np.asarray(inputs["L0_attn_bqkv"]).reshape(48, 128).T),
        "Wo": np.ascontiguousarray(np.asarray(inputs["L0_attn_Wo"])),
        "bo": np.ascontiguousarray(
            np.asarray(inputs["L0_attn_bo"]).reshape(16, 128).T),
        "Wg": np.ascontiguousarray(np.asarray(inputs["L0_ffn_gate_w"])),
        "Wu": np.ascontiguousarray(np.asarray(inputs["L0_ffn_up_w"])),
        "Wd": np.ascontiguousarray(np.asarray(inputs["L0_ffn_down_w"])),
        "gs": np.ascontiguousarray(
            np.asarray(inputs["L0_ffn_gate_scale"]).reshape(48, 128).T),
        "us": np.ascontiguousarray(
            np.asarray(inputs["L0_ffn_up_scale"]).reshape(48, 128).T),
        "ds": np.ascontiguousarray(
            np.asarray(inputs["L0_ffn_down_scale"]).reshape(16, 128).T),
        "idn": np.eye(128, dtype=ml_dtypes.bfloat16),
        "idnf": np.eye(128, dtype=np.float32),
        "onesb": np.ones((128, 128), dtype=ml_dtypes.bfloat16),
        "onesf": np.ones((128, 128), dtype=np.float32),
    }

    in_maps = []
    for c in range(8):
        b, s = c // 2, c % 2
        blks = blocks[s]
        xs = np.concatenate([x[b, g * 512:(g + 1) * 512, :] for g in blks], 0)
        mask = np.zeros((T, 1024), dtype=np.float32)
        q_glob = np.concatenate(
            [np.arange(g * 512, (g + 1) * 512) for g in blks])
        for s_ in range(4):
            rows = slice(s_ * 512, (s_ + 1) * 512)
            qg = q_glob[rows][:, None]
            ka = np.arange(A_BLOCKS[s_] * 512,
                           (A_BLOCKS[s_] + 1) * 512)[None, :]
            kb = np.arange(B_BLOCKS[s_] * 512,
                           (B_BLOCKS[s_] + 1) * 512)[None, :]
            mask[rows, 0:512] = np.where(ka <= qg, 0.0, MSK_VAL)
            mask[rows, 512:1024] = np.where(kb <= qg, 0.0, MSK_VAL)
        lo = (c // 2) * 2
        m = dict(base)
        m["x"] = np.ascontiguousarray(xs, dtype=np.float32)
        m["mask"] = mask.astype(ml_dtypes.bfloat16)
        m["idxa"] = (lo * 512 + np.arange(128, dtype=np.uint32)
                     ).reshape(128, 1)
        m["idxb"] = ((lo + 1) * 512 + np.arange(128, dtype=np.uint32)
                     ).reshape(128, 1)
        in_maps.append(m)
    return in_maps


def assemble_output(results):
    res = np.zeros((4, 4096, 2048), dtype=np.float32)
    blocks = {0: A_BLOCKS, 1: B_BLOCKS}
    for c in range(8):
        b, s = c // 2, c % 2
        y = np.asarray(results[c]["out"])
        for i, g in enumerate(blocks[s]):
            res[b, g * 512:(g + 1) * 512, :] = y[i * 512:(i + 1) * 512, :]
    return res


def kernel(**inputs):
    nc = _get_nc(3)
    in_maps = make_core_inputs(inputs)
    res = run_bass_kernel_spmd(nc, in_maps, core_ids=list(range(8)))
    return assemble_output(res.results)


# version 06
# speedup vs baseline: 1.0119x; 1.0119x over previous
import math
from contextlib import ExitStack

import numpy as np
import ml_dtypes

import concourse.bass as bass
import concourse.mybir as mybir
from concourse.bass import IndirectOffsetOnAxis
from concourse.bass_utils import run_bass_kernel_spmd

F32 = mybir.dt.float32
BF16 = mybir.dt.bfloat16
FP8 = mybir.dt.float8e4
U32 = mybir.dt.uint32
AF = mybir.ActivationFunctionType
ALU = mybir.AluOpType

T = 2048
FDM = 2048
ID = 6144
NH = 16
NTC = 16
NFC = 16
NIC = 48
MSK_VAL = -10000.0 * math.sqrt(128.0)
A_BLOCKS = [0, 3, 4, 7]
B_BLOCKS = [1, 2, 5, 6]

N_WAVES = [1, 1, 2, 2]
N_CHUNKS = [2, 4, 6, 8]
N_KC = [8, 16, 24, 32]


class Ctr:
    def __init__(self, sem):
        self.sem = sem
        self.n = 0

    def inc(self, ins, k=1):
        ins.then_inc(self.sem, k)
        self.n += k
        return self.n


def build_program(stages=3, do_conv=True, cut=None):
    nc = bass.Bass("TRN2", target_bir_lowering=False)

    xin = nc.dram_tensor("x", [T, FDM], F32, kind="ExternalInput").ap()
    Wqkv = nc.dram_tensor("Wqkv", [FDM, 3 * FDM], F32, kind="ExternalInput").ap()
    bqkv_in = nc.dram_tensor("bqkv", [128, 48], F32, kind="ExternalInput").ap()
    Wo = nc.dram_tensor("Wo", [FDM, FDM], F32, kind="ExternalInput").ap()
    bo_in = nc.dram_tensor("bo", [128, 16], F32, kind="ExternalInput").ap()
    Wg = nc.dram_tensor("Wg", [FDM, ID], F32, kind="ExternalInput").ap()
    Wu = nc.dram_tensor("Wu", [FDM, ID], F32, kind="ExternalInput").ap()
    Wd = nc.dram_tensor("Wd", [ID, FDM], F32, kind="ExternalInput").ap()
    gs_in = nc.dram_tensor("gs", [128, 48], F32, kind="ExternalInput").ap()
    us_in = nc.dram_tensor("us", [128, 48], F32, kind="ExternalInput").ap()
    ds_in = nc.dram_tensor("ds", [128, 16], F32, kind="ExternalInput").ap()
    mask_in = nc.dram_tensor("mask", [T, 1024], BF16, kind="ExternalInput").ap()
    idxa_in = nc.dram_tensor("idxa", [128, 1], U32, kind="ExternalInput").ap()
    idxb_in = nc.dram_tensor("idxb", [128, 1], U32, kind="ExternalInput").ap()
    idn_in = nc.dram_tensor("idn", [128, 128], BF16, kind="ExternalInput").ap()
    idnf_in = nc.dram_tensor("idnf", [128, 128], F32, kind="ExternalInput").ap()
    onesb_in = nc.dram_tensor("onesb", [128, 128], BF16, kind="ExternalInput").ap()
    onesf_in = nc.dram_tensor("onesf", [128, 128], F32, kind="ExternalInput").ap()
    out = nc.dram_tensor("out", [T, FDM], F32, kind="ExternalOutput").ap()

    xa = nc.dram_tensor("xa", [T, FDM], F32).ap()
    xb = nc.dram_tensor("xb", [T, FDM], F32).ap()
    W8g = nc.dram_tensor("W8g", [NIC, 128, FDM], FP8).ap()
    W8u = nc.dram_tensor("W8u", [NIC, 128, FDM], FP8).ap()
    W8d = nc.dram_tensor("W8d", [NFC, 128, ID], FP8).ap()
    Qst = nc.dram_tensor("Qst", [FDM, T], BF16).ap()
    Ost = nc.dram_tensor("Ost", [FDM, T], BF16).ap()
    contribKs = [nc.dram_tensor("contribK%d" % g, [512, T], BF16).ap()
                 for g in range(4)]
    contribVs = [nc.dram_tensor("contribV%d" % g, [512, T], BF16).ap()
                 for g in range(4)]
    gatheredKs = [nc.dram_tensor("gatheredK%d" % g, [8 * 512, T], BF16).ap()
                  for g in range(4)]
    gatheredVs = [nc.dram_tensor("gatheredV%d" % g, [8 * 512, T], BF16).ap()
                  for g in range(4)]

    gctx = ExitStack()
    _uid = [0]

    def _nm(n):
        _uid[0] += 1
        return "%s_%d" % (n, _uid[0])

    sb = lambda n, s, d: gctx.enter_context(nc.sbuf_tensor(n, s, d))
    PS = gctx.enter_context(nc.psum_tensor("PS", [128, 4096], F32))
    bank = lambda i: PS[:, i * 512:(i + 1) * 512]

    idn = sb("c_idn", [128, 128], BF16)
    idnf = sb("c_idnf", [128, 128], F32)
    onesb = sb("c_onesb", [128, 128], BF16)
    onesf = sb("c_onesf", [128, 128], F32)
    c_gs = sb("c_gs", [128, 48], F32)
    c_us = sb("c_us", [128, 48], F32)
    c_dsh = sb("c_dsh", [128, 16], F32)
    c_bqkv = sb("c_bqkv", [128, 48], F32)
    c_bo = sb("c_bo", [128, 16], F32)
    c_isq = sb("c_isq", [128, 1], F32)
    c_eps = sb("c_eps", [128, 1], F32)
    c_tiny = sb("c_tiny", [128, 1], F32)
    c_half = sb("c_half", [128, 1], F32)
    c_mhalf = sb("c_mhalf", [128, 1], F32)
    c_mone = sb("c_mone", [128, 1], F32)
    c_mln128 = sb("c_mln128", [128, 1], F32)
    idxa = sb("c_idxa", [128, 1], U32)
    idxb = sb("c_idxb", [128, 1], U32)

    sPE = Ctr(gctx.enter_context(nc.semaphore("sPE")))
    sACT = Ctr(gctx.enter_context(nc.semaphore("sACT")))
    sDVE = Ctr(gctx.enter_context(nc.semaphore("sDVE")))
    sCC = Ctr(gctx.enter_context(nc.semaphore("sCC")))
    DS = {}

    def dsem(name):
        if name not in DS:
            DS[name] = Ctr(gctx.enter_context(nc.semaphore(name)))
        return DS[name]

    const_loads = [(idn, idn_in), (idnf, idnf_in), (onesb, onesb_in),
                   (onesf, onesf_in), (c_gs, gs_in), (c_us, us_in),
                   (c_dsh, ds_in), (c_bqkv, bqkv_in), (c_bo, bo_in),
                   (idxa, idxa_in), (idxb, idxb_in)]
    with nc.Block() as blk:
        @blk.sync
        def _(sync):
            for dst, src in const_loads:
                dsem("const").inc(sync.dma_start(out=dst[:], in_=src[:]), 16)

        @blk.vector
        def _(vector):
            vector.memset(c_isq[:], 1.0 / math.sqrt(128.0))
            vector.memset(c_eps[:], 1e-5)
            vector.memset(c_tiny[:], 1e-30)
            vector.memset(c_half[:], 0.5)
            vector.memset(c_mhalf[:], -0.5)
            vector.memset(c_mone[:], -1.0)
            vector.memset(c_mln128[:], -0.5 * math.log(128.0))
            vector.wait_ge(dsem("const").sem, dsem("const").n)
            sDVE.inc(vector.engine_nop())

        @blk.gpsimd
        def _(gpsimd):
            if do_conv:
                for ic in range(NIC):
                    for w_src, w_dst in [(Wg, W8g), (Wu, W8u)]:
                        src = w_src[:, ic * 128:(ic + 1) * 128].rearrange(
                            "(fc p) i -> p fc i", p=128)
                        dst = w_dst[ic].rearrange("p (fc i) -> p fc i", i=128)
                        dsem("cvt").inc(gpsimd.dma_start(out=dst, in_=src), 16)
                for hc in range(NFC):
                    for part in range(3):
                        src = Wd[part * 2048:(part + 1) * 2048,
                                 hc * 128:(hc + 1) * 128].rearrange(
                            "(ic p) h -> p ic h", p=128)
                        dst = W8d[hc].rearrange(
                            "p (ic h) -> p ic h", h=128)[
                            :, part * 16:(part + 1) * 16, :]
                        dsem("cvt").inc(gpsimd.dma_start(out=dst, in_=src), 16)

    with nc.Block() as blk:
        @blk.vector
        def _(vector):
            sDVE.inc(vector.tensor_scalar(c_dsh[:], c_dsh[:], 0.5, None,
                                          ALU.mult))

    cvt_total = dsem("cvt").n

    def emit_ln(x_src, xn_ft, lctx, src_ready=None, parts="AB", entry=None):
        lsb = lambda n, s, d: lctx.enter_context(nc.sbuf_tensor(_nm(n), s, d))
        xt = [lsb("ln_x0", [128, FDM], F32), lsb("ln_x1", [128, FDM], F32)]
        sq = lsb("ln_sq", [128, FDM], F32)
        sq2 = lsb("ln_sqb", [128, FDM], F32)
        st_s = lsb("ln_ss", [128, NTC], F32)
        st_q = lsb("ln_sq2", [128, NTC], F32)
        aa = lsb("ln_a", [128, NTC], F32)
        bb = lsb("ln_b", [128, NTC], F32)
        xnt = [lsb("ln_xn0", [128, FDM], BF16), lsb("ln_xn1", [128, FDM], BF16)]
        lnx = [dsem("lnx0"), dsem("lnx1")]

        lx0 = [lnx[0].n, lnx[1].n]
        act0_ = sACT.n
        with nc.Block() as blk:
            @blk.scalar
            def _(scalar):
                for tc in range(NTC):
                    scalar.wait_ge(lnx[tc % 2].sem,
                                   lx0[tc % 2] + 16 * (tc // 2 + 1))
                    if tc >= 1:
                        scalar.wait_ge(sACT.sem, act0_ + 2 * tc)
                    sACT.inc(scalar.activation(
                        sq[:], xt[tc % 2][:], AF.Copy, bias=0.0, scale=1.0,
                        accum_out=st_s[:, tc:tc + 1]))
                    sACT.inc(scalar.activation(
                        sq2[:], xt[tc % 2][:], AF.Square, bias=0.0, scale=1.0,
                        accum_out=st_q[:, tc:tc + 1]))

            @blk.sync
            def _(sync):
                if entry is not None:
                    for c_, v_ in entry:
                        sync.wait_ge(c_.sem, v_)
                for tc in range(NTC):
                    if src_ready is not None:
                        for c_, v_ in src_ready(tc):
                            sync.wait_ge(c_.sem, v_)
                    if tc >= 2:
                        sync.wait_ge(sACT.sem, act0_ + 2 * (tc - 1))
                    lnx[tc % 2].inc(sync.dma_start(
                        out=xt[tc % 2][:],
                        in_=x_src[tc * 128:(tc + 1) * 128, :]), 16)
        assert sACT.n == act0_ + 2 * NTC

        dveS0, actS0 = sDVE.n, sACT.n
        with nc.Block() as blk:
            @blk.vector
            def _(vector):
                d = dveS0
                vector.wait_ge(sACT.sem, actS0)
                sDVE.inc(vector.tensor_scalar(aa[:], st_s[:], 1.0 / FDM,
                                              None, ALU.mult)); d += 1
                vector.wait_ge(sDVE.sem, d)
                sDVE.inc(vector.scalar_tensor_tensor(
                    bb[:], aa[:], 1.0, aa[:], ALU.mult, ALU.mult)); d += 1
                vector.wait_ge(sDVE.sem, d)
                sDVE.inc(vector.tensor_scalar(st_q[:], st_q[:], 1.0 / FDM,
                                              None, ALU.mult)); d += 1
                vector.wait_ge(sDVE.sem, d)
                sDVE.inc(vector.scalar_tensor_tensor(
                    st_q[:], st_q[:], 1.0, bb[:], ALU.mult,
                    ALU.subtract)); d += 1
                vector.wait_ge(sACT.sem, actS0 + 2)
                sDVE.inc(vector.scalar_tensor_tensor(
                    bb[:], aa[:], 1.0, st_q[:], ALU.mult, ALU.mult)); d += 1
                vector.wait_ge(sDVE.sem, d)
                sDVE.inc(vector.tensor_scalar(bb[:], bb[:], -1.0, None,
                                              ALU.mult)); d += 1
                vector.wait_ge(sDVE.sem, d)
                sDVE.inc(vector.tensor_copy(aa[:], st_q[:]))

            @blk.scalar
            def _(scalar):
                scalar.wait_ge(sDVE.sem, dveS0 + 4)
                sACT.inc(scalar.activation(st_q[:], st_q[:], AF.Ln,
                                           bias=c_eps[:, 0:1], scale=1.0))
                scalar.wait_ge(sACT.sem, actS0 + 1)
                sACT.inc(scalar.activation(st_q[:], st_q[:], AF.Exp, bias=0.0,
                                           scale=c_mhalf[:, 0:1]))
        assert sDVE.n == dveS0 + 7 and sACT.n == actS0 + 2

        if parts == "A":
            return
        do_pe = parts in ("B1", "AB")
        do_evac = parts == "AB"
        lx1 = [lnx[0].n, lnx[1].n]
        dve1, act1, pe1 = sDVE.n, sACT.n, sPE.n
        with nc.Block() as blk:
            @blk.tensor
            def _(tensor):
                if not do_pe:
                    return
                for tc in range(NTC):
                    tensor.wait_ge(sDVE.sem, dve1 + tc * 9 + 1)
                    if tc >= 2 and do_evac:
                        tensor.wait_ge(sDVE.sem, dve1 + (tc - 2) * 9 + 9)
                        tensor.wait_ge(sACT.sem, act1 + (tc - 1) * 8)
                    b0 = 0 if tc % 2 == 0 else 4
                    for fc in range(NFC):
                        col = (b0 + fc // 4) * 512 + (fc % 4) * 128
                        mm = tensor.matmul(
                            PS[:, col:col + 128],
                            xnt[tc % 2][:, fc * 128:(fc + 1) * 128],
                            idn[:], start=True, stop=True)
                    sPE.inc(mm)

            @blk.scalar
            def _(scalar):
                if not do_evac:
                    return
                for tc in range(NTC):
                    scalar.wait_ge(sPE.sem, pe1 + tc + 1)
                    b0 = 0 if tc % 2 == 0 else 4
                    for fc in range(8, NFC):
                        col = (b0 + fc // 4) * 512 + (fc % 4) * 128
                        sACT.inc(scalar.activation(
                            xn_ft[:, fc * T + tc * 128: fc * T + tc * 128 + 128],
                            PS[:, col:col + 128], AF.Copy, bias=0.0, scale=1.0))

            @blk.vector
            def _(vector):
                for tc in range(NTC):
                    vector.wait_ge(lnx[tc % 2].sem,
                                   lx1[tc % 2] + 16 * (tc // 2 + 1))
                    if tc >= 2 and do_pe:
                        vector.wait_ge(sPE.sem, pe1 + tc - 1)
                    sDVE.inc(vector.tensor_scalar(
                        xnt[tc % 2][:], xt[tc % 2][:], aa[:, tc:tc + 1],
                        bb[:, tc:tc + 1], ALU.mult, ALU.add))
                    if not do_evac:
                        for fc in range(0, NFC, 2):
                            sDVE.inc(vector.engine_nop())
                        continue
                    vector.wait_ge(sPE.sem, pe1 + tc + 1)
                    b0 = 0 if tc % 2 == 0 else 4
                    for fc in range(0, 8):
                        col = (b0 + fc // 4) * 512 + (fc % 4) * 128
                        sDVE.inc(vector.tensor_copy(
                            xn_ft[:, fc * T + tc * 128: fc * T + tc * 128 + 128],
                            PS[:, col:col + 128]))

            @blk.sync
            def _(sync):
                for tc in range(NTC):
                    if src_ready is not None:
                        for c_, v_ in src_ready(tc):
                            sync.wait_ge(c_.sem, v_)
                    if tc >= 2:
                        sync.wait_ge(sDVE.sem, dve1 + (tc - 2) * 9 + 1)
                    lnx[tc % 2].inc(sync.dma_start(
                        out=xt[tc % 2][:],
                        in_=x_src[tc * 128:(tc + 1) * 128, :]), 16)
        if parts == "AB":
            assert sDVE.n == dve1 + 9 * NTC
            assert sACT.n == act1 + 8 * NTC
            assert sPE.n == pe1 + NTC

    def emit_ffn(x_src, x_dst, xn_ft, fctx, src_ready=None, first=False, entry=None):
        fsb = lambda n, s, d: fctx.enter_context(nc.sbuf_tensor(_nm(n), s, d))
        fuse = fsb("f_fuse", [128, NIC * 512], BF16)
        wg2 = [fsb("f_wg0", [128, FDM], BF16), fsb("f_wg1", [128, FDM], BF16)]
        wu2 = [fsb("f_wu0", [128, FDM], BF16), fsb("f_wu1", [128, FDM], BF16)]
        wd2 = [fsb("f_wd0", [128, ID], BF16), fsb("f_wd1", [128, ID], BF16)]
        gsil = fsb("f_gsil", [128, 512], F32)
        uu = fsb("f_uu", [128, 512], F32)
        yst = fsb("f_yst", [128, 512], F32)
        xnew = [fsb("f_xn%d" % s, [128, FDM], F32) for s in range(4)]
        swg = [dsem("wg0"), dsem("wg1")]
        swu = [dsem("wu0"), dsem("wu1")]
        swd = [dsem("wd0"), dsem("wd1")]
        sxn = [dsem("fxn%d" % s) for s in range(4)]
        sxs = [dsem("fxs%d" % s) for s in range(4)]

        prefetched = False
        for tb in range(4):
            pe0, act0, dve0 = sPE.n, sACT.n, sDVE.n
            wgb = [swg[0].n, swg[1].n]
            wub = [swu[0].n, swu[1].n]
            wdb = [swd[0].n, swd[1].n]
            if prefetched:
                wgb = [v - 16 for v in wgb]
                wub = [v - 16 for v in wub]
            skip_ic = 2 if prefetched else 0
            xnb = [s_.n for s_ in sxn]
            xsb = [s_.n for s_ in sxs]
            pe_gu = lambda ic: pe0 + ic + 1
            pe_dn = lambda hc: pe0 + NIC + 2 * hc + 1
            pe_yt = lambda hc: pe0 + NIC + 2 * hc + 2
            act_ic = lambda ic: act0 + ic + 1
            dve_ic = lambda ic: dve0 + ic + 1
            dve_yst = lambda hc: dve0 + NIC + 2 * hc + 1
            dve_add = lambda hc: dve0 + NIC + 2 * hc + 2

            with nc.Block() as blk:
                @blk.tensor
                def _(tensor):
                    for ic in range(NIC):
                        tensor.wait_ge(swg[ic % 2].sem,
                                       wgb[ic % 2] + 16 * (ic // 2 + 1))
                        tensor.wait_ge(swu[ic % 2].sem,
                                       wub[ic % 2] + 16 * (ic // 2 + 1))
                        if ic >= 2:
                            tensor.wait_ge(sACT.sem, act_ic(ic - 2))
                        for f in range(NFC):
                            tensor.matmul(
                                bank(ic % 2),
                                wg2[ic % 2][:, f * 128:(f + 1) * 128],
                                xn_ft[:, f * T + tb * 512: f * T + tb * 512 + 512],
                                start=(f == 0), stop=(f == NFC - 1))
                        for f in range(NFC):
                            mm = tensor.matmul(
                                bank(2 + ic % 2),
                                wu2[ic % 2][:, f * 128:(f + 1) * 128],
                                xn_ft[:, f * T + tb * 512: f * T + tb * 512 + 512],
                                start=(f == 0), stop=(f == NFC - 1))
                        sPE.inc(mm)
                    for hc in range(NFC):
                        tensor.wait_ge(swd[hc % 2].sem,
                                       wdb[hc % 2] + 16 * (hc // 2 + 1))
                        if hc >= 2:
                            tensor.wait_ge(sDVE.sem, dve_yst(hc - 2))
                        for ic in range(NIC):
                            if hc == 0 and ic == 0:
                                tensor.wait_ge(sDVE.sem, dve_ic(NIC - 1))
                            mm = tensor.matmul(
                                bank(4 + hc % 2),
                                wd2[hc % 2][:, ic * 128:(ic + 1) * 128],
                                fuse[:, ic * 512:(ic + 1) * 512],
                                start=(ic == 0), stop=(ic == NIC - 1))
                        sPE.inc(mm)
                        tensor.wait_ge(sDVE.sem, dve_yst(hc))
                        if hc >= 2:
                            tensor.wait_ge(sDVE.sem, dve_add(hc - 2))
                        for s in range(4):
                            col = (6 + hc % 2) * 512 + s * 128
                            mm = tensor.matmul(PS[:, col:col + 128],
                                               yst[:, s * 128:(s + 1) * 128],
                                               idnf[:], start=True, stop=True)
                        sPE.inc(mm)

                @blk.scalar
                def _(scalar):
                    for ic in range(NIC):
                        scalar.wait_ge(sPE.sem, pe_gu(ic))
                        if ic >= 1:
                            scalar.wait_ge(sDVE.sem, dve_ic(ic - 1))
                        scalar.activation(gsil[:], bank(ic % 2), AF.Silu,
                                          bias=0.0, scale=c_gs[:, ic:ic + 1])
                        sACT.inc(scalar.activation(
                            uu[:], bank(2 + ic % 2), AF.Copy, bias=0.0,
                            scale=c_us[:, ic:ic + 1]))

                @blk.vector
                def _(vector):
                    for ic in range(NIC):
                        vector.wait_ge(sACT.sem, act_ic(ic))
                        sDVE.inc(vector.scalar_tensor_tensor(
                            fuse[:, ic * 512:(ic + 1) * 512], gsil[:], 1.0,
                            uu[:], ALU.mult, ALU.mult))
                    for hc in range(NFC):
                        vector.wait_ge(sPE.sem, pe_dn(hc))
                        sDVE.inc(vector.tensor_scalar(
                            yst[:], bank(4 + hc % 2), c_dsh[:, hc:hc + 1],
                            None, ALU.mult))
                        vector.wait_ge(sPE.sem, pe_yt(hc))
                        if hc == 0:
                            for s in range(4):
                                vector.wait_ge(sxn[s].sem, xnb[s] + 16)
                        for s in range(4):
                            col = (6 + hc % 2) * 512 + s * 128
                            vv = vector.scalar_tensor_tensor(
                                xnew[s][:, hc * 128:(hc + 1) * 128],
                                PS[:, col:col + 128], 1.0,
                                xnew[s][:, hc * 128:(hc + 1) * 128],
                                ALU.mult, ALU.add)
                        sDVE.inc(vv)

                @blk.sync
                def _(sync):
                    if tb == 0 and entry is not None:
                        for c_, v_ in entry:
                            sync.wait_ge(c_.sem, v_)
                    for s in range(4):
                        tc = tb * 4 + s
                        if src_ready is not None:
                            for c_, v_ in src_ready(tc):
                                sync.wait_ge(c_.sem, v_)
                        sync.wait_ge(sxs[s].sem, xsb[s])
                        sxn[s].inc(sync.dma_start(
                            out=xnew[s][:],
                            in_=x_src[tc * 128:(tc + 1) * 128, :]), 16)
                    sync.wait_ge(sDVE.sem, dve_add(NFC - 1))
                    for s in range(4):
                        tc = tb * 4 + s
                        sxs[s].inc(sync.dma_start(
                            out=x_dst[tc * 128:(tc + 1) * 128, :],
                            in_=xnew[s][:]), 16)

                @blk.gpsimd
                def _(gpsimd):
                    if tb == 0 and entry is not None:
                        for c_, v_ in entry:
                            gpsimd.wait_ge(c_.sem, v_)
                    for ic in range(skip_ic, NIC):
                        if first and tb == 0 and ic < 2:
                            gpsimd.wait_ge(dsem("cvt").sem, cvt_total)
                        if ic >= 2:
                            gpsimd.wait_ge(sPE.sem, pe_gu(ic - 2))
                        swg[ic % 2].inc(gpsimd.dma_start(out=wg2[ic % 2][:],
                                                         in_=W8g[ic]), 16)
                        swu[ic % 2].inc(gpsimd.dma_start(out=wu2[ic % 2][:],
                                                         in_=W8u[ic]), 16)
                    for hc in range(NFC):
                        if hc >= 2:
                            gpsimd.wait_ge(sPE.sem, pe_dn(hc - 2))
                        swd[hc % 2].inc(gpsimd.dma_start(out=wd2[hc % 2][:],
                                                         in_=W8d[hc]), 16)
                        if hc == 7 and tb < 3:
                            for icn in range(2):
                                swg[icn].inc(gpsimd.dma_start(
                                    out=wg2[icn][:], in_=W8g[icn]), 16)
                                swu[icn].inc(gpsimd.dma_start(
                                    out=wu2[icn][:], in_=W8u[icn]), 16)

            assert sPE.n == pe0 + NIC + 2 * NFC
            assert sACT.n == act0 + NIC
            assert sDVE.n == dve0 + NIC + 2 * NFC
            prefetched = (tb < 3)
        return sxs

    def emit_qkv(xn_ft, qctx):
        qsb = lambda n, s, d: qctx.enter_context(nc.sbuf_tensor(_nm(n), s, d))
        wq2 = [qsb("q_w0", [128, FDM], BF16), qsb("q_w1", [128, FDM], BF16)]
        qkv2 = [qsb("q_s0", [128, FDM], BF16), qsb("q_s1", [128, FDM], BF16)]
        vstage = [qsb("q_v0", [128, FDM], BF16), qsb("q_v1", [128, FDM], BF16)]
        swq = [dsem("qw0"), dsem("qw1")]
        sqst = [dsem("qst0"), dsem("qst1")]
        svst = [dsem("vst0"), dsem("vst1")]

        order = []
        for g in range(4):
            order += list(range(16 + 4 * g, 16 + 4 * g + 4))
            order += list(range(32 + 4 * g, 32 + 4 * g + 4))
            order += list(range(4 * g, 4 * g + 4))
        pe0, act0, dve0 = sPE.n, sACT.n, sDVE.n
        wqb = [swq[0].n, swq[1].n]
        qstb = [sqst[0].n, sqst[1].n]
        vstb = [svst[0].n, svst[1].n]

        pe_sched = []
        acc = pe0
        for j, qc in enumerate(order):
            acc += 1
            mm_inc = acc
            t_inc = None
            if qc >= 32:
                acc += 1
                t_inc = acc
            pe_sched.append((mm_inc, t_inc))
        act_j = lambda j: act0 + j + 1
        dve_sched = []
        acc = dve0
        for j, qc in enumerate(order):
            if qc >= 32:
                acc += 1
                dve_sched.append(acc)
            else:
                dve_sched.append(None)
        qst_cnt = [0, 0]
        vst_cnt = [0, 0]
        st_at = {}
        for j, qc in enumerate(order):
            p = j % 2
            if qc >= 32:
                vst_cnt[p] += 16
                st_at[j] = (svst[p], vstb[p] + vst_cnt[p])
            else:
                qst_cnt[p] += 16
                st_at[j] = (sqst[p], qstb[p] + qst_cnt[p])

        with nc.Block() as blk:
            @blk.tensor
            def _(tensor):
                for j, qc in enumerate(order):
                    base = 0 if j % 2 == 0 else 2048
                    tensor.wait_ge(swq[j % 2].sem,
                                   wqb[j % 2] + 16 * (j // 2 + 1))
                    if j >= 2:
                        tensor.wait_ge(sACT.sem, act_j(j - 2))
                        if order[j - 2] >= 32:
                            tensor.wait_ge(sDVE.sem, dve_sched[j - 2])
                    for f in range(NFC):
                        for tq in range(4):
                            mm = tensor.matmul(
                                PS[:, base + tq * 512: base + tq * 512 + 512],
                                wq2[j % 2][:, f * 128:(f + 1) * 128],
                                xn_ft[:, f * T + tq * 512: f * T + tq * 512 + 512],
                                start=(f == 0), stop=(f == NFC - 1))
                    sPE.inc(mm)
                    if qc >= 32:
                        tensor.wait_ge(sACT.sem, act_j(j))
                        for tcg in range(16):
                            col = base + (tcg // 4) * 512 + (tcg % 4) * 128
                            mm = tensor.matmul(
                                PS[:, col:col + 128],
                                qkv2[j % 2][:, tcg * 128:(tcg + 1) * 128],
                                idn[:], start=True, stop=True)
                        sPE.inc(mm)

            @blk.scalar
            def _(scalar):
                for j, qc in enumerate(order):
                    base = 0 if j % 2 == 0 else 2048
                    scalar.wait_ge(sPE.sem, pe_sched[j][0])
                    if j >= 2:
                        c_, v_ = st_at[j - 2]
                        scalar.wait_ge(c_.sem, v_)
                    sACT.inc(scalar.activation(
                        qkv2[j % 2][:], PS[:, base:base + 2048], AF.Identity,
                        bias=c_bqkv[:, qc:qc + 1], scale=1.0))

            @blk.vector
            def _(vector):
                for j, qc in enumerate(order):
                    if qc < 32:
                        continue
                    base = 0 if j % 2 == 0 else 2048
                    vector.wait_ge(sPE.sem, pe_sched[j][1])
                    if j >= 2 and order[j - 2] >= 32:
                        c_, v_ = st_at[j - 2]
                        vector.wait_ge(c_.sem, v_)
                    for g in range(4):
                        vv = vector.tensor_copy(
                            vstage[j % 2][:, g * 512:(g + 1) * 512],
                            PS[:, base + g * 512: base + g * 512 + 512])
                    sDVE.inc(vv)

            @blk.sync
            def _(sync):
                for j, qc in enumerate(order):
                    head = qc % 16
                    hg, hr = head // 4, head % 4
                    if qc >= 32:
                        sync.wait_ge(sDVE.sem, dve_sched[j])
                        st_at[j][0].inc(sync.dma_start(
                            out=contribVs[hg][hr * 128:(hr + 1) * 128, :],
                            in_=vstage[j % 2][:]), 16)
                    elif qc >= 16:
                        sync.wait_ge(sACT.sem, act_j(j))
                        st_at[j][0].inc(sync.dma_start(
                            out=contribKs[hg][hr * 128:(hr + 1) * 128, :],
                            in_=qkv2[j % 2][:]), 16)
                    else:
                        sync.wait_ge(sACT.sem, act_j(j))
                        st_at[j][0].inc(sync.dma_start(
                            out=Qst[head * 128:(head + 1) * 128, :],
                            in_=qkv2[j % 2][:]), 16)

            @blk.gpsimd
            def _(gpsimd):
                for j, qc in enumerate(order):
                    if j >= 2:
                        gpsimd.wait_ge(sPE.sem, pe_sched[j - 2][0])
                    src = Wqkv[:, qc * 128:(qc + 1) * 128].rearrange(
                        "(fc p) q -> p fc q", p=128)
                    dst = wq2[j % 2][:].rearrange("p (fc q) -> p fc q", q=128)
                    swq[j % 2].inc(gpsimd.dma_start(out=dst, in_=src), 16)
                    if j % 12 == 7:
                        g = j // 12
                        gpsimd.wait_ge(sqst[0].sem, qstb[0] + 64 * g + 32)
                        gpsimd.wait_ge(sqst[1].sem, qstb[1] + 64 * g + 32)
                        sCC.inc(gpsimd.collective_compute(
                            "AllGather", ALU.bypass,
                            replica_groups=[list(range(8))],
                            ins=[contribKs[g].opt()],
                            outs=[gatheredKs[g].opt()]), 1)
                    if j % 12 == 11:
                        g = j // 12
                        gpsimd.wait_ge(svst[0].sem, vstb[0] + 32 * (g + 1))
                        gpsimd.wait_ge(svst[1].sem, vstb[1] + 32 * (g + 1))
                        sCC.inc(gpsimd.collective_compute(
                            "AllGather", ALU.bypass,
                            replica_groups=[list(range(8))],
                            ins=[contribVs[g].opt()],
                            outs=[gatheredVs[g].opt()]), 1)

        assert sACT.n == act0 + 48
        q_store_at = {order[j]: st_at[j] for j in range(48) if order[j] < 16}
        qkv_store_totals = [(sqst[0], sqst[0].n), (sqst[1], sqst[1].n),
                            (svst[0], svst[0].n), (svst[1], svst[1].n)]
        return q_store_at, qkv_store_totals

    def emit_attn(actx, dmabufs, q_store_at, qkv_store_totals, entry=None):
        asb = lambda n, s, d: actx.enter_context(nc.sbuf_tensor(_nm(n), s, d))
        Ksb, Vsb, Qsb, msk = dmabufs
        Psb = [asb("a_p%d" % q, [128, 4096], BF16) for q in range(4)]
        PTs = [asb("a_pt%d" % i, [128, 512], BF16) for i in range(4)]
        Bsb = asb("a_B", [128, 512], F32)
        Ohd = [asb("a_o0", [128, FDM], BF16), asb("a_o1", [128, FDM], BF16)]
        K2 = asb("a_k2", [128, 4096], BF16)
        Q2 = asb("a_q2", [128, FDM], BF16)
        mxc = asb("a_mx", [128, 8], F32)
        nrm = asb("a_nrm", [128, 16], F32)
        negm = asb("a_negm", [128, 16], F32)
        sgp = asb("a_sgp", [128, 64], F32)
        sg = asb("a_sg", [128, 4], F32)
        rsgT = asb("a_rsgT", [1, 512], F32)
        skk = [dsem("kk0"), dsem("kk1")]
        svv = [dsem("vv0"), dsem("vv1")]
        sqq = [dsem("qq0"), dsem("qq1")]
        soh = [dsem("oh0"), dsem("oh1")]
        smk = dsem("mskS")

        with nc.Block() as blk:
            @blk.sync
            def _(sync):
                if entry is not None:
                    for c_, v_ in entry:
                        sync.wait_ge(c_.sem, v_)
                for s in range(4):
                    src = mask_in[s * 512:(s + 1) * 512, :].rearrange(
                        "(qs p) k -> p qs k", p=128)
                    dst = msk[s][:].rearrange("p (qs k) -> p qs k", k=1024)
                    smk.inc(sync.dma_start(out=dst, in_=src), 16)
        mask_total = smk.n

        kb0 = [skk[0].n, skk[1].n]
        vb0 = [svv[0].n, svv[1].n]
        qb0 = [sqq[0].n, sqq[1].n]
        ob0 = [soh[0].n, soh[1].n]

        HS = []
        pe, act, dve = sPE.n, sACT.n, sDVE.n
        for h in range(NH):
            peS, actS, dveS = {}, {}, {}
            pe += 1; peS["mxw0"] = pe
            pe += 1; peS["mxw1"] = pe
            pe += 1; peS["q2"] = pe
            dve += 1; dveS["k2q2"] = dve
            dve += 1; dveS["mxr0"] = dve
            dve += 1; dveS["mxr1"] = dve
            dve += 1; dveS["mxf"] = dve
            dve += 1; dveS["tprod"] = dve
            act += 2; actS["mest"] = act
            dve += 1; dveS["negm"] = dve
            wave_keys = []
            for s in range(4):
                for q in range(4):
                    for wi in range(s + 1):
                        pe += 1; peS[("S", s, q, wi)] = pe
                        act += 1; actS[("exp", s, q, wi)] = act
                        wave_keys.append((s, q, wi))
                dve += 1; dveS[("sig", s)] = dve
                act += 2; actS[("rsg", s)] = act
                for kc in range(N_KC[s]):
                    pe += 1; peS[("T", s, kc)] = pe
                    if kc % 4 < 3:
                        dve += 1; dveS[("PTe", s, kc)] = dve
                    else:
                        act += 1; actS[("PTe", s, kc)] = act
                pe += 1; peS[("rsgT", s)] = pe
                dve += 1; dveS[("rsgTe", s)] = dve
                pe += 1; peS[("Bmm", s)] = pe
                dve += 1; dveS[("Be", s)] = dve
                pe += 1; peS[("AV", s)] = pe
                dve += 1; dveS[("Oe", s)] = dve
            HS.append((peS, actS, dveS, wave_keys))

        guard = {b: None for b in range(8)}

        def bank_waits(eng, banks):
            done = set()
            for b in banks:
                g = guard[b]
                if g is not None and g not in done:
                    eng.wait_ge(g[0], g[1])
                    done.add(g)
                guard[b] = None

        with nc.Block() as blk:
            @blk.tensor
            def _(tensor):
                for h in range(NH):
                    peS, actS, dveS, wave_keys = HS[h]
                    tensor.wait_ge(sDVE.sem, dveS["k2q2"])
                    for w in range(2):
                        if w == 1:
                            tensor.wait_ge(sDVE.sem, dveS["mxr0"])
                        bank_waits(tensor, [0, 1, 2, 3])
                        for g in range(4):
                            mm = tensor.matmul(bank(g), onesb[:],
                                               K2[:, (w * 4 + g) * 512:
                                                  (w * 4 + g + 1) * 512],
                                               start=True, stop=True)
                        sPE.inc(mm)
                        for g in range(4):
                            guard[g] = (sDVE.sem, dveS["mxr%d" % w])
                    tensor.wait_ge(sDVE.sem, dveS["mxr1"])
                    bank_waits(tensor, [7])
                    for qs in range(16):
                        mm = tensor.matmul(
                            PS[:, 7 * 512 + qs: 7 * 512 + qs + 1],
                            Q2[:, qs * 128:(qs + 1) * 128],
                            onesb[:, 0:1], start=True, stop=True)
                    sPE.inc(mm)
                    guard[7] = (sDVE.sem, dveS["tprod"])
                    widx = 0
                    for s in range(4):
                        for q in range(4):
                            qt = s * 4 + q
                            for wi in range(s + 1):
                                bp = widx % 4
                                tensor.wait_ge(sDVE.sem, dveS["negm"])
                                if h == 0 and widx == 0:
                                    tensor.wait_ge(smk.sem, mask_total)
                                if widx >= 3:
                                    tensor.wait_ge(
                                        sACT.sem,
                                        actS[("exp",) + wave_keys[widx - 3]])
                                bank_waits(tensor, [2 * bp, 2 * bp + 1])
                                for ci in range(2):
                                    c = wi * 2 + ci
                                    if c <= s:
                                        kap = Ksb[h % 2][:, c * 512:
                                                         (c + 1) * 512]
                                    else:
                                        cb = c - s - 1
                                        kap = Ksb[h % 2][:, 2048 + cb * 512:
                                                         2048 + cb * 512 + 512]
                                    has_mask = (c == s or c == 2 * s + 1)
                                    mm = tensor.matmul(
                                        PS[:, (2 * bp + ci) * 512:
                                           (2 * bp + ci) * 512 + 512],
                                        Qsb[h % 2][:, qt * 128:(qt + 1) * 128],
                                        kap, start=True, stop=not has_mask)
                                    if has_mask:
                                        moff = 0 if c == s else 512
                                        mm = tensor.matmul(
                                            PS[:, (2 * bp + ci) * 512:
                                               (2 * bp + ci) * 512 + 512],
                                            idn[:],
                                            msk[s][:, q * 1024 + moff:
                                                   q * 1024 + moff + 512],
                                            start=False, stop=True)
                                sPE.inc(mm)
                                gv = (sACT.sem, actS[("exp", s, q, wi)])
                                guard[2 * bp] = gv
                                guard[2 * bp + 1] = gv
                                widx += 1
                        nkc = N_KC[s]
                        na = 4 * (s + 1)

                        def emit_av(kc, nkc=nkc, na=na, s=s, h=h,
                                    dveS=dveS, actS=actS):
                            who = ("PTe", s, kc)
                            if kc % 4 < 3:
                                tensor.wait_ge(sDVE.sem, dveS[who])
                            else:
                                tensor.wait_ge(sACT.sem, actS[who])
                            if kc == 0:
                                bank_waits(tensor, [6])
                            if kc < na:
                                vap = Vsb[h % 2][:, kc * 128:(kc + 1) * 128]
                            else:
                                vap = Vsb[h % 2][:, 2048 + (kc - na) * 128:
                                                 2048 + (kc - na) * 128 + 128]
                            return tensor.matmul(bank(6), vap,
                                                 PTs[kc % 4][:],
                                                 start=(kc == 0),
                                                 stop=(kc == nkc - 1))

                        for kc in range(nkc):
                            if kc == 0:
                                tensor.wait_ge(sACT.sem,
                                               actS[("exp", s, 3, s)])
                            bnk = kc % 4
                            bank_waits(tensor, [bnk])
                            for q in range(4):
                                col = bnk * 512 + q * 128
                                mm = tensor.matmul(
                                    PS[:, col:col + 128],
                                    Psb[q][:, kc * 128:(kc + 1) * 128],
                                    idn[:], start=True, stop=True)
                            sPE.inc(mm)
                            who = ("PTe", s, kc)
                            guard[bnk] = ((sDVE.sem, dveS[who])
                                          if kc % 4 < 3
                                          else (sACT.sem, actS[who]))
                            if kc >= 2:
                                mm = emit_av(kc - 2)
                        tensor.wait_ge(sDVE.sem, dveS[("sig", s)])
                        tensor.wait_ge(sACT.sem, actS[("rsg", s)])
                        bank_waits(tensor, [7])
                        for q in range(4):
                            mm = tensor.matmul(
                                PS[0:1, 7 * 512 + q * 128:
                                   7 * 512 + q * 128 + 128],
                                sg[:, q:q + 1], idnf[:],
                                start=True, stop=True)
                        sPE.inc(mm)
                        guard[7] = (sDVE.sem, dveS[("rsgTe", s)])
                        tensor.wait_ge(sDVE.sem, dveS[("rsgTe", s)])
                        bank_waits(tensor, [7])
                        for q in range(4):
                            mm = tensor.matmul(
                                PS[:, 7 * 512 + q * 128:
                                   7 * 512 + (q + 1) * 128],
                                onesf[0:1, :],
                                rsgT[0:1, q * 128:(q + 1) * 128],
                                start=True, stop=True)
                        sPE.inc(mm)
                        guard[7] = (sDVE.sem, dveS[("Be", s)])
                        mm = emit_av(nkc - 2)
                        mm = emit_av(nkc - 1)
                        sPE.inc(mm)
                        guard[6] = (sDVE.sem, dveS[("Oe", s)])

            @blk.scalar
            def _(scalar):
                for h in range(NH):
                    peS, actS, dveS, wave_keys = HS[h]
                    scalar.wait_ge(sDVE.sem, dveS["tprod"])
                    sACT.inc(scalar.activation(nrm[:], nrm[:], AF.Ln,
                                               bias=c_tiny[:, 0:1], scale=1.0))
                    scalar.wait_ge(sACT.sem, actS["mest"] - 1)
                    sACT.inc(scalar.activation(nrm[:], nrm[:], AF.Exp,
                                               bias=c_mln128[:, 0:1],
                                               scale=c_half[:, 0:1]))
                    widx = 0
                    for s in range(4):
                        for q in range(4):
                            qt = s * 4 + q
                            for wi in range(s + 1):
                                bp = widx % 4
                                scalar.wait_ge(sPE.sem, peS[("S", s, q, wi)])
                                if h >= 1 and widx == 0:
                                    scalar.wait_ge(
                                        sPE.sem,
                                        HS[h - 1][0][("T", 3, N_KC[3] - 1)])
                                sACT.inc(scalar.activation(
                                    Psb[q][:, wi * 1024: wi * 1024 + 1024],
                                    PS[:, bp * 1024: bp * 1024 + 1024],
                                    AF.Exp, bias=negm[:, qt:qt + 1],
                                    scale=c_isq[:, 0:1],
                                    accum_out=sgp[:, qt * 4 + wi:
                                                  qt * 4 + wi + 1]))
                                widx += 1
                        scalar.wait_ge(sDVE.sem, dveS[("sig", s)])
                        sACT.inc(scalar.activation(sg[:], sg[:], AF.Ln,
                                                   bias=c_tiny[:, 0:1],
                                                   scale=1.0))
                        scalar.wait_ge(sACT.sem, actS[("rsg", s)] - 1)
                        sACT.inc(scalar.activation(sg[:], sg[:], AF.Exp,
                                                   bias=0.0,
                                                   scale=c_mone[:, 0:1]))
                        for kc in range(N_KC[s]):
                            if kc % 4 < 3:
                                continue
                            scalar.wait_ge(sPE.sem, peS[("T", s, kc)])
                            sACT.inc(scalar.activation(
                                PTs[kc % 4][:],
                                PS[:, (kc % 4) * 512:(kc % 4) * 512 + 512],
                                AF.Copy, bias=0.0, scale=1.0))

            @blk.vector
            def _(vector):
                for h in range(NH):
                    peS, actS, dveS, wave_keys = HS[h]
                    kv_val = 16 * 2 * (h // 2 + 1)
                    vector.wait_ge(skk[h % 2].sem, kb0[h % 2] + kv_val)
                    vector.wait_ge(svv[h % 2].sem, vb0[h % 2] + kv_val)
                    vector.wait_ge(sqq[h % 2].sem,
                                   qb0[h % 2] + 16 * (h // 2 + 1))
                    vector.scalar_tensor_tensor(
                        K2[:], Ksb[h % 2][:], 1.0, Ksb[h % 2][:],
                        ALU.mult, ALU.mult)
                    sDVE.inc(vector.scalar_tensor_tensor(
                        Q2[:], Qsb[h % 2][:], 1.0, Qsb[h % 2][:],
                        ALU.mult, ALU.mult))
                    for w in range(2):
                        vector.wait_ge(sPE.sem, peS["mxw%d" % w])
                        for g in range(4):
                            vv = vector.reduce_max(
                                mxc[:, w * 4 + g:w * 4 + g + 1],
                                bank(g), mybir.AxisListType.X)
                        sDVE.inc(vv)
                    vector.wait_ge(sDVE.sem, dveS["mxr1"])
                    sDVE.inc(vector.reduce_max(mxc[:, 0:1], mxc[:, 0:8],
                                               mybir.AxisListType.X))
                    vector.wait_ge(sDVE.sem, dveS["mxf"])
                    vector.wait_ge(sPE.sem, peS["q2"])
                    sDVE.inc(vector.tensor_scalar(
                        nrm[:], PS[:, 7 * 512: 7 * 512 + 16], mxc[:, 0:1],
                        None, ALU.mult))
                    vector.wait_ge(sACT.sem, actS["mest"])
                    sDVE.inc(vector.tensor_scalar(
                        negm[:], nrm[:], -1.0, None, ALU.mult))
                    for s in range(4):
                        nw = s + 1
                        vector.wait_ge(sACT.sem, actS[("exp", s, 3, nw - 1)])
                        for q in range(4):
                            qt = s * 4 + q
                            if nw == 1:
                                vv = vector.tensor_copy(
                                    sg[:, q:q + 1],
                                    sgp[:, qt * 4:qt * 4 + 1])
                            else:
                                vv = vector.reduce_sum(
                                    sg[:, q:q + 1],
                                    sgp[:, qt * 4:qt * 4 + nw],
                                    mybir.AxisListType.X)
                        sDVE.inc(vv)
                        for kc in range(N_KC[s]):
                            if kc % 4 >= 3:
                                continue
                            vector.wait_ge(sPE.sem, peS[("T", s, kc)])
                            sDVE.inc(vector.tensor_copy(
                                PTs[kc % 4][:],
                                PS[:, (kc % 4) * 512:(kc % 4) * 512 + 512]))
                        vector.wait_ge(sPE.sem, peS[("rsgT", s)])
                        sDVE.inc(vector.tensor_copy(
                            rsgT[0:1, :], PS[0:1, 7 * 512: 7 * 512 + 512]))
                        vector.wait_ge(sPE.sem, peS[("Bmm", s)])
                        sDVE.inc(vector.tensor_copy(
                            Bsb[:], PS[:, 7 * 512: 7 * 512 + 512]))
                        vector.wait_ge(sPE.sem, peS[("AV", s)])
                        vector.wait_ge(sDVE.sem, dveS[("Be", s)])
                        if h >= 2 and s == 0:
                            vector.wait_ge(soh[h % 2].sem,
                                           ob0[h % 2] + 16 * (h // 2))
                        sDVE.inc(vector.scalar_tensor_tensor(
                            Ohd[h % 2][:, s * 512:(s + 1) * 512],
                            bank(6), 1.0, Bsb[:], ALU.mult, ALU.mult))

            @blk.sync
            def _(sync):
                for h in range(NH):
                    peS, actS, dveS, wave_keys = HS[h]
                    if h >= 2:
                        sync.wait_ge(sPE.sem, HS[h - 2][0][("AV", 3)])
                    sync.wait_ge(q_store_at[h][0].sem, q_store_at[h][1])
                    sqq[h % 2].inc(sync.dma_start(
                        out=Qsb[h % 2][:],
                        in_=Qst[h * 128:(h + 1) * 128, :]), 16)
                    sync.wait_ge(sDVE.sem, dveS[("Oe", 3)])
                    soh[h % 2].inc(sync.dma_start(
                        out=Ost[h * 128:(h + 1) * 128, :],
                        in_=Ohd[h % 2][:]), 16)

            @blk.gpsimd
            def _(gpsimd):
                for h in range(NH):
                    hg, hr = h // 4, h % 4
                    if hr == 0:
                        gpsimd.wait_ge(sCC.sem, 2 * hg + 1)
                    if h >= 2:
                        gpsimd.wait_ge(sPE.sem, HS[h - 2][0][("AV", 3)])
                    for rank, idx in [(0, idxa), (1, idxb)]:
                        skk[h % 2].inc(gpsimd.indirect_dma_start(
                            out=Ksb[h % 2][:, rank * 2048: rank * 2048 + 2048],
                            out_offset=None, in_=gatheredKs[hg],
                            in_offset=IndirectOffsetOnAxis(ap=idx[:, 0:1],
                                                           axis=0),
                            element_offset=(hr * 128) * T), 16)
                    if hr == 0:
                        gpsimd.wait_ge(sCC.sem, 2 * hg + 2)
                    for rank, idx in [(0, idxa), (1, idxb)]:
                        svv[h % 2].inc(gpsimd.indirect_dma_start(
                            out=Vsb[h % 2][:, rank * 2048: rank * 2048 + 2048],
                            out_offset=None, in_=gatheredVs[hg],
                            in_offset=IndirectOffsetOnAxis(ap=idx[:, 0:1],
                                                           axis=0),
                            element_offset=(hr * 128) * T), 16)

        assert sPE.n == pe and sACT.n == act and sDVE.n == dve
        oh_totals = [(soh[0], soh[0].n), (soh[1], soh[1].n)]
        return oh_totals

    def emit_outproj(x_src, x_dst, octx, oh_totals, src_ready):
        osb = lambda n, s, d: octx.enter_context(nc.sbuf_tensor(_nm(n), s, d))
        wo2 = [osb("o_w0", [128, FDM], BF16), osb("o_w1", [128, FDM], BF16)]
        Ot = [osb("o_t0", [128, 16 * 512], BF16),
              osb("o_t1", [128, 16 * 512], BF16)]
        yst = osb("o_yst", [128, 512], F32)
        xnew = [osb("o_xn%d" % s, [128, FDM], F32) for s in range(4)]
        swo = [dsem("ow0"), dsem("ow1")]
        sot = [dsem("oto0"), dsem("oto1")]
        sxn = [dsem("fxn%d" % s) for s in range(4)]
        sxs = [dsem("fxs%d" % s) for s in range(4)]

        for tb in range(4):
            pe0, act0, dve0 = sPE.n, sACT.n, sDVE.n
            wob = [swo[0].n, swo[1].n]
            otb = [sot[0].n, sot[1].n]
            xnb = [s_.n for s_ in sxn]
            xsb = [s_.n for s_ in sxs]
            pe_mm = lambda hc: pe0 + 2 * hc + 1
            pe_yt = lambda hc: pe0 + 2 * hc + 2
            dve_y = lambda hc: dve0 + 2 * hc + 1
            dve_a = lambda hc: dve0 + 2 * hc + 2

            with nc.Block() as blk:
                @blk.tensor
                def _(tensor):
                    for hc in range(NFC):
                        tensor.wait_ge(swo[hc % 2].sem,
                                       wob[hc % 2] + 16 * (hc // 2 + 1))
                        if hc == 0:
                            tensor.wait_ge(sot[tb % 2].sem, otb[tb % 2] + 16)
                        if hc >= 2:
                            tensor.wait_ge(sDVE.sem, dve_y(hc - 2))
                        for oc in range(NFC):
                            mm = tensor.matmul(
                                bank(hc % 2),
                                wo2[hc % 2][:, oc * 128:(oc + 1) * 128],
                                Ot[tb % 2][:, oc * 512:(oc + 1) * 512],
                                start=(oc == 0), stop=(oc == NFC - 1))
                        sPE.inc(mm)
                        tensor.wait_ge(sDVE.sem, dve_y(hc))
                        if hc >= 2:
                            tensor.wait_ge(sDVE.sem, dve_a(hc - 2))
                        for s in range(4):
                            col = (2 + hc % 2) * 512 + s * 128
                            mm = tensor.matmul(PS[:, col:col + 128],
                                               yst[:, s * 128:(s + 1) * 128],
                                               idnf[:], start=True, stop=True)
                        sPE.inc(mm)

                @blk.vector
                def _(vector):
                    for hc in range(NFC):
                        vector.wait_ge(sPE.sem, pe_mm(hc))
                        if hc >= 1:
                            vector.wait_ge(sPE.sem, pe_yt(hc - 1))
                        sDVE.inc(vector.tensor_scalar(
                            yst[:], bank(hc % 2), c_bo[:, hc:hc + 1],
                            None, ALU.add))
                        vector.wait_ge(sPE.sem, pe_yt(hc))
                        if hc == 0:
                            for s in range(4):
                                vector.wait_ge(sxn[s].sem, xnb[s] + 16)
                        for s in range(4):
                            col = (2 + hc % 2) * 512 + s * 128
                            vv = vector.scalar_tensor_tensor(
                                xnew[s][:, hc * 128:(hc + 1) * 128],
                                PS[:, col:col + 128], 1.0,
                                xnew[s][:, hc * 128:(hc + 1) * 128],
                                ALU.mult, ALU.add)
                        sDVE.inc(vv)

                @blk.sync
                def _(sync):
                    if tb < 2:
                        for c_, v_ in oh_totals:
                            sync.wait_ge(c_.sem, v_)
                    sync.wait_ge(sot[tb % 2].sem, otb[tb % 2])
                    sot[tb % 2].inc(sync.dma_start(
                        out=Ot[tb % 2][:],
                        in_=Ost[:, tb * 512:(tb + 1) * 512].rearrange(
                            "(oc p) t -> p oc t", p=128)), 16)
                    for s in range(4):
                        tc = tb * 4 + s
                        for c_, v_ in src_ready(tc):
                            sync.wait_ge(c_.sem, v_)
                        sync.wait_ge(sxs[s].sem, xsb[s])
                        sxn[s].inc(sync.dma_start(
                            out=xnew[s][:],
                            in_=x_src[tc * 128:(tc + 1) * 128, :]), 16)
                    sync.wait_ge(sDVE.sem, dve_a(NFC - 1))
                    for s in range(4):
                        tc = tb * 4 + s
                        sxs[s].inc(sync.dma_start(
                            out=x_dst[tc * 128:(tc + 1) * 128, :],
                            in_=xnew[s][:]), 16)

                @blk.gpsimd
                def _(gpsimd):
                    for hc in range(NFC):
                        if hc >= 2:
                            gpsimd.wait_ge(sPE.sem, pe_mm(hc - 2))
                        src = Wo[:, hc * 128:(hc + 1) * 128].rearrange(
                            "(oc p) h -> p oc h", p=128)
                        dst = wo2[hc % 2][:].rearrange("p (oc h) -> p oc h",
                                                       h=128)
                        swo[hc % 2].inc(gpsimd.dma_start(out=dst, in_=src), 16)

            assert sPE.n == pe0 + 2 * NFC
            assert sDVE.n == dve0 + 2 * NFC
        return sxs

    def store_ready(sxs_list, base_counts):
        def f(tc):
            s = tc % 4
            k = tc // 4
            return [(sxs_list[s], base_counts[s] + 16 * (k + 1))]
        return f

    def sublayer_ffn(x_src, x_dst, src_ready=None, first=False):
        entry = (None if src_ready is None else
                 [src_ready(12 + i)[0] for i in range(4)])
        with ExitStack() as fctx:
            xn_ft = fctx.enter_context(
                nc.sbuf_tensor(_nm("xn_ft"), [128, NFC * T], BF16))
            with ExitStack() as lctx:
                emit_ln(x_src, xn_ft, lctx, src_ready, entry=entry)
            base = [dsem("fxs%d" % s).n for s in range(4)]
            sxs = emit_ffn(x_src, x_dst, xn_ft, fctx, src_ready, first,
                           entry=entry)
        return store_ready(sxs, base)

    def sublayer_attn(x_src, x_dst, src_ready):
        with ExitStack() as octx0:
            ksb_ = [octx0.enter_context(nc.sbuf_tensor(_nm("a_k%d" % i),
                                                       [128, 4096], BF16))
                    for i in range(2)]
            vsb_ = [octx0.enter_context(nc.sbuf_tensor(_nm("a_v%d" % i),
                                                       [128, 4096], BF16))
                    for i in range(2)]
            qsb_ = [octx0.enter_context(nc.sbuf_tensor(_nm("a_q%d" % i),
                                                       [128, FDM], BF16))
                    for i in range(2)]
            msk_ = [octx0.enter_context(nc.sbuf_tensor(_nm("a_m%d" % i),
                                                       [128, 4096], BF16))
                    for i in range(4)]
            entry = (None if src_ready is None else
                     [src_ready(12 + i)[0] for i in range(4)])
            with ExitStack() as qctx:
                xn_ft = qctx.enter_context(
                    nc.sbuf_tensor(_nm("xn_ft"), [128, NFC * T], BF16))
                with ExitStack() as lctx:
                    emit_ln(x_src, xn_ft, lctx, src_ready, entry=entry)
                q_store_at, qkv_totals = emit_qkv(xn_ft, qctx)
            with ExitStack() as actx:
                oh_totals = emit_attn(actx, (ksb_, vsb_, qsb_, msk_),
                                      q_store_at, qkv_totals, entry=entry)
        with ExitStack() as octx:
            base = [dsem("fxs%d" % s).n for s in range(4)]
            sxs = emit_outproj(x_src, x_dst, octx, oh_totals, src_ready)
        return store_ready(sxs, base)

    if cut is not None and cut.startswith("ln"):
        parts = cut[2:] if len(cut) > 2 else "AB"
        with ExitStack() as fctx:
            xn_ft = fctx.enter_context(
                nc.sbuf_tensor(_nm("xn_ft"), [128, NFC * T], BF16))
            with ExitStack() as lctx:
                emit_ln(xin, xn_ft, lctx, None, parts)
    elif stages == 1:
        sublayer_ffn(xin, out, None, first=True)
    elif stages == 2:
        r1 = sublayer_ffn(xin, xa, None, first=True)
        sublayer_attn(xa, out, r1)
    else:
        r1 = sublayer_ffn(xin, xa, None, first=True)
        r2 = sublayer_attn(xa, xb, r1)
        sublayer_ffn(xb, out, r2, first=False)

    with nc.Block() as blk:
        @blk.sync
        def _(sync):
            for nm, c in DS.items():
                sync.wait_ge(c.sem, c.n)

    gctx.close()
    return nc


_CACHE = {}


def _get_nc(stages):
    if stages not in _CACHE:
        _CACHE[stages] = build_program(stages)
    return _CACHE[stages]


def make_core_inputs(inputs):
    x = np.asarray(inputs["L0_attn_x"])
    blocks = {0: A_BLOCKS, 1: B_BLOCKS}

    base = {
        "Wqkv": np.ascontiguousarray(np.asarray(inputs["L0_attn_Wqkv"])),
        "bqkv": np.ascontiguousarray(
            np.asarray(inputs["L0_attn_bqkv"]).reshape(48, 128).T),
        "Wo": np.ascontiguousarray(np.asarray(inputs["L0_attn_Wo"])),
        "bo": np.ascontiguousarray(
            np.asarray(inputs["L0_attn_bo"]).reshape(16, 128).T),
        "Wg": np.ascontiguousarray(np.asarray(inputs["L0_ffn_gate_w"])),
        "Wu": np.ascontiguousarray(np.asarray(inputs["L0_ffn_up_w"])),
        "Wd": np.ascontiguousarray(np.asarray(inputs["L0_ffn_down_w"])),
        "gs": np.ascontiguousarray(
            np.asarray(inputs["L0_ffn_gate_scale"]).reshape(48, 128).T),
        "us": np.ascontiguousarray(
            np.asarray(inputs["L0_ffn_up_scale"]).reshape(48, 128).T),
        "ds": np.ascontiguousarray(
            np.asarray(inputs["L0_ffn_down_scale"]).reshape(16, 128).T),
        "idn": np.eye(128, dtype=ml_dtypes.bfloat16),
        "idnf": np.eye(128, dtype=np.float32),
        "onesb": np.ones((128, 128), dtype=ml_dtypes.bfloat16),
        "onesf": np.ones((128, 128), dtype=np.float32),
    }

    in_maps = []
    for c in range(8):
        b, s = c // 2, c % 2
        blks = blocks[s]
        xs = np.concatenate([x[b, g * 512:(g + 1) * 512, :] for g in blks], 0)
        mask = np.zeros((T, 1024), dtype=np.float32)
        q_glob = np.concatenate(
            [np.arange(g * 512, (g + 1) * 512) for g in blks])
        for s_ in range(4):
            rows = slice(s_ * 512, (s_ + 1) * 512)
            qg = q_glob[rows][:, None]
            ka = np.arange(A_BLOCKS[s_] * 512,
                           (A_BLOCKS[s_] + 1) * 512)[None, :]
            kb = np.arange(B_BLOCKS[s_] * 512,
                           (B_BLOCKS[s_] + 1) * 512)[None, :]
            mask[rows, 0:512] = np.where(ka <= qg, 0.0, MSK_VAL)
            mask[rows, 512:1024] = np.where(kb <= qg, 0.0, MSK_VAL)
        lo = (c // 2) * 2
        m = dict(base)
        m["x"] = np.ascontiguousarray(xs, dtype=np.float32)
        m["mask"] = mask.astype(ml_dtypes.bfloat16)
        m["idxa"] = (lo * 512 + np.arange(128, dtype=np.uint32)
                     ).reshape(128, 1)
        m["idxb"] = ((lo + 1) * 512 + np.arange(128, dtype=np.uint32)
                     ).reshape(128, 1)
        in_maps.append(m)
    return in_maps


def assemble_output(results):
    res = np.zeros((4, 4096, 2048), dtype=np.float32)
    blocks = {0: A_BLOCKS, 1: B_BLOCKS}
    for c in range(8):
        b, s = c // 2, c % 2
        y = np.asarray(results[c]["out"])
        for i, g in enumerate(blocks[s]):
            res[b, g * 512:(g + 1) * 512, :] = y[i * 512:(i + 1) * 512, :]
    return res


def kernel(**inputs):
    nc = _get_nc(3)
    in_maps = make_core_inputs(inputs)
    res = run_bass_kernel_spmd(nc, in_maps, core_ids=list(range(8)))
    return assemble_output(res.results)


# version 07 (final)
# speedup vs baseline: 1.0345x; 1.0223x over previous
import math
from contextlib import ExitStack

import numpy as np
import ml_dtypes

import concourse.bass as bass
import concourse.mybir as mybir
from concourse.bass import IndirectOffsetOnAxis
from concourse.bass_utils import run_bass_kernel_spmd

F32 = mybir.dt.float32
BF16 = mybir.dt.bfloat16
FP8 = mybir.dt.float8e4
U32 = mybir.dt.uint32
AF = mybir.ActivationFunctionType
ALU = mybir.AluOpType

T = 2048
FDM = 2048
ID = 6144
NH = 16
NTC = 16
NFC = 16
NIC = 48
MSK_VAL = -10000.0 * math.sqrt(128.0)
A_BLOCKS = [0, 3, 4, 7]
B_BLOCKS = [1, 2, 5, 6]

N_WAVES = [1, 1, 2, 2]
N_CHUNKS = [2, 4, 6, 8]
N_KC = [8, 16, 24, 32]


class Ctr:
    def __init__(self, sem):
        self.sem = sem
        self.n = 0

    def inc(self, ins, k=1):
        ins.then_inc(self.sem, k)
        self.n += k
        return self.n


def build_program(stages=3, do_conv=True, cut=None):
    nc = bass.Bass("TRN2", target_bir_lowering=False)

    xin = nc.dram_tensor("x", [T, FDM], F32, kind="ExternalInput").ap()
    Wqkv = nc.dram_tensor("Wqkv", [FDM, 3 * FDM], F32, kind="ExternalInput").ap()
    bqkv_in = nc.dram_tensor("bqkv", [128, 48], F32, kind="ExternalInput").ap()
    Wo = nc.dram_tensor("Wo", [FDM, FDM], F32, kind="ExternalInput").ap()
    bo_in = nc.dram_tensor("bo", [128, 16], F32, kind="ExternalInput").ap()
    Wg = nc.dram_tensor("Wg", [FDM, ID], F32, kind="ExternalInput").ap()
    Wu = nc.dram_tensor("Wu", [FDM, ID], F32, kind="ExternalInput").ap()
    Wd = nc.dram_tensor("Wd", [ID, FDM], F32, kind="ExternalInput").ap()
    gs_in = nc.dram_tensor("gs", [128, 48], F32, kind="ExternalInput").ap()
    us_in = nc.dram_tensor("us", [128, 48], F32, kind="ExternalInput").ap()
    ds_in = nc.dram_tensor("ds", [128, 16], F32, kind="ExternalInput").ap()
    mask_in = nc.dram_tensor("mask", [T, 1024], BF16, kind="ExternalInput").ap()
    idxa_in = nc.dram_tensor("idxa", [128, 1], U32, kind="ExternalInput").ap()
    idxb_in = nc.dram_tensor("idxb", [128, 1], U32, kind="ExternalInput").ap()
    idn_in = nc.dram_tensor("idn", [128, 128], BF16, kind="ExternalInput").ap()
    idnf_in = nc.dram_tensor("idnf", [128, 128], F32, kind="ExternalInput").ap()
    onesb_in = nc.dram_tensor("onesb", [128, 128], BF16, kind="ExternalInput").ap()
    onesf_in = nc.dram_tensor("onesf", [128, 128], F32, kind="ExternalInput").ap()
    out = nc.dram_tensor("out", [T, FDM], F32, kind="ExternalOutput").ap()

    xa = nc.dram_tensor("xa", [T, FDM], F32).ap()
    xb = nc.dram_tensor("xb", [T, FDM], F32).ap()
    W8g = nc.dram_tensor("W8g", [NIC, 128, FDM], FP8).ap()
    W8u = nc.dram_tensor("W8u", [NIC, 128, FDM], FP8).ap()
    W8d = nc.dram_tensor("W8d", [NFC, 128, ID], FP8).ap()
    Qst = nc.dram_tensor("Qst", [FDM, T], BF16).ap()
    Ost = nc.dram_tensor("Ost", [FDM, T], BF16).ap()
    WoB = nc.dram_tensor("WoB", [FDM, FDM], BF16).ap()
    contribKs = [nc.dram_tensor("contribK%d" % g, [512, T], BF16).ap()
                 for g in range(4)]
    contribVs = [nc.dram_tensor("contribV%d" % g, [512, T], BF16).ap()
                 for g in range(4)]
    gatheredKs = [nc.dram_tensor("gatheredK%d" % g, [8 * 512, T], BF16,
                                 addr_space="Shared").ap()
                  for g in range(4)]
    gatheredVs = [nc.dram_tensor("gatheredV%d" % g, [8 * 512, T], BF16,
                                 addr_space="Shared").ap()
                  for g in range(4)]

    gctx = ExitStack()
    _uid = [0]

    def _nm(n):
        _uid[0] += 1
        return "%s_%d" % (n, _uid[0])

    sb = lambda n, s, d: gctx.enter_context(nc.sbuf_tensor(n, s, d))
    PS = gctx.enter_context(nc.psum_tensor("PS", [128, 4096], F32))
    bank = lambda i: PS[:, i * 512:(i + 1) * 512]

    idn = sb("c_idn", [128, 128], BF16)
    idnf = sb("c_idnf", [128, 128], F32)
    onesb = sb("c_onesb", [128, 128], BF16)
    onesf = sb("c_onesf", [128, 128], F32)
    c_gs = sb("c_gs", [128, 48], F32)
    c_us = sb("c_us", [128, 48], F32)
    c_dsh = sb("c_dsh", [128, 16], F32)
    c_bqkv = sb("c_bqkv", [128, 48], F32)
    c_bo = sb("c_bo", [128, 16], F32)
    c_isq = sb("c_isq", [128, 1], F32)
    c_eps = sb("c_eps", [128, 1], F32)
    c_tiny = sb("c_tiny", [128, 1], F32)
    c_half = sb("c_half", [128, 1], F32)
    c_mhalf = sb("c_mhalf", [128, 1], F32)
    c_mone = sb("c_mone", [128, 1], F32)
    c_mln128 = sb("c_mln128", [128, 1], F32)
    idxa = sb("c_idxa", [128, 1], U32)
    idxb = sb("c_idxb", [128, 1], U32)

    sPE = Ctr(gctx.enter_context(nc.semaphore("sPE")))
    sACT = Ctr(gctx.enter_context(nc.semaphore("sACT")))
    sDVE = Ctr(gctx.enter_context(nc.semaphore("sDVE")))
    sCC = Ctr(gctx.enter_context(nc.semaphore("sCC")))
    DS = {}

    def dsem(name):
        if name not in DS:
            DS[name] = Ctr(gctx.enter_context(nc.semaphore(name)))
        return DS[name]

    const_loads = [(idn, idn_in), (idnf, idnf_in), (onesb, onesb_in),
                   (onesf, onesf_in), (c_gs, gs_in), (c_us, us_in),
                   (c_dsh, ds_in), (c_bqkv, bqkv_in), (c_bo, bo_in),
                   (idxa, idxa_in), (idxb, idxb_in)]
    with nc.Block() as blk:
        @blk.sync
        def _(sync):
            for dst, src in const_loads:
                dsem("const").inc(sync.dma_start(out=dst[:], in_=src[:]), 16)

        @blk.vector
        def _(vector):
            vector.memset(c_isq[:], 1.0 / math.sqrt(128.0))
            vector.memset(c_eps[:], 1e-5)
            vector.memset(c_tiny[:], 1e-30)
            vector.memset(c_half[:], 0.5)
            vector.memset(c_mhalf[:], -0.5)
            vector.memset(c_mone[:], -1.0)
            vector.memset(c_mln128[:], -0.5 * math.log(128.0))
            vector.wait_ge(dsem("const").sem, dsem("const").n)
            sDVE.inc(vector.engine_nop())

        @blk.gpsimd
        def _(gpsimd):
            if do_conv:
                for ic in range(NIC):
                    for w_src, w_dst in [(Wg, W8g), (Wu, W8u)]:
                        src = w_src[:, ic * 128:(ic + 1) * 128].rearrange(
                            "(fc p) i -> p fc i", p=128)
                        dst = w_dst[ic].rearrange("p (fc i) -> p fc i", i=128)
                        dsem("cvt").inc(gpsimd.dma_start(out=dst, in_=src), 16)
                for hc in range(NFC):
                    for part in range(3):
                        src = Wd[part * 2048:(part + 1) * 2048,
                                 hc * 128:(hc + 1) * 128].rearrange(
                            "(ic p) h -> p ic h", p=128)
                        dst = W8d[hc].rearrange(
                            "p (ic h) -> p ic h", h=128)[
                            :, part * 16:(part + 1) * 16, :]
                        dsem("cvt").inc(gpsimd.dma_start(out=dst, in_=src), 16)
                dsem("woc").inc(gpsimd.dma_start(out=WoB[:], in_=Wo[:]), 16)

    with nc.Block() as blk:
        @blk.vector
        def _(vector):
            sDVE.inc(vector.tensor_scalar(c_dsh[:], c_dsh[:], 0.5, None,
                                          ALU.mult))

    cvt_total = dsem("cvt").n

    def emit_ln(x_src, xn_ft, lctx, src_ready=None, parts="AB", entry=None):
        lsb = lambda n, s, d: lctx.enter_context(nc.sbuf_tensor(_nm(n), s, d))
        xt = [lsb("ln_x0", [128, FDM], F32), lsb("ln_x1", [128, FDM], F32)]
        sq = lsb("ln_sq", [128, FDM], F32)
        sq2 = lsb("ln_sqb", [128, FDM], F32)
        st_s = lsb("ln_ss", [128, NTC], F32)
        st_q = lsb("ln_sq2", [128, NTC], F32)
        aa = lsb("ln_a", [128, NTC], F32)
        bb = lsb("ln_b", [128, NTC], F32)
        xnt = [lsb("ln_xn0", [128, FDM], BF16), lsb("ln_xn1", [128, FDM], BF16)]
        lnx = [dsem("lnx0"), dsem("lnx1")]

        lx0 = [lnx[0].n, lnx[1].n]
        act0_ = sACT.n
        with nc.Block() as blk:
            @blk.scalar
            def _(scalar):
                for tc in range(NTC):
                    scalar.wait_ge(lnx[tc % 2].sem,
                                   lx0[tc % 2] + 16 * (tc // 2 + 1))
                    if tc >= 1:
                        scalar.wait_ge(sACT.sem, act0_ + 2 * tc)
                    sACT.inc(scalar.activation(
                        sq[:], xt[tc % 2][:], AF.Copy, bias=0.0, scale=1.0,
                        accum_out=st_s[:, tc:tc + 1]))
                    sACT.inc(scalar.activation(
                        sq2[:], xt[tc % 2][:], AF.Square, bias=0.0, scale=1.0,
                        accum_out=st_q[:, tc:tc + 1]))

            @blk.sync
            def _(sync):
                if entry is not None:
                    for c_, v_ in entry:
                        sync.wait_ge(c_.sem, v_)
                for tc in range(NTC):
                    if src_ready is not None:
                        for c_, v_ in src_ready(tc):
                            sync.wait_ge(c_.sem, v_)
                    if tc >= 2:
                        sync.wait_ge(sACT.sem, act0_ + 2 * (tc - 1))
                    lnx[tc % 2].inc(sync.dma_start(
                        out=xt[tc % 2][:],
                        in_=x_src[tc * 128:(tc + 1) * 128, :]), 16)
        assert sACT.n == act0_ + 2 * NTC

        dveS0, actS0 = sDVE.n, sACT.n
        with nc.Block() as blk:
            @blk.vector
            def _(vector):
                d = dveS0
                vector.wait_ge(sACT.sem, actS0)
                sDVE.inc(vector.tensor_scalar(aa[:], st_s[:], 1.0 / FDM,
                                              None, ALU.mult)); d += 1
                vector.wait_ge(sDVE.sem, d)
                sDVE.inc(vector.scalar_tensor_tensor(
                    bb[:], aa[:], 1.0, aa[:], ALU.mult, ALU.mult)); d += 1
                vector.wait_ge(sDVE.sem, d)
                sDVE.inc(vector.tensor_scalar(st_q[:], st_q[:], 1.0 / FDM,
                                              None, ALU.mult)); d += 1
                vector.wait_ge(sDVE.sem, d)
                sDVE.inc(vector.scalar_tensor_tensor(
                    st_q[:], st_q[:], 1.0, bb[:], ALU.mult,
                    ALU.subtract)); d += 1
                vector.wait_ge(sACT.sem, actS0 + 2)
                sDVE.inc(vector.scalar_tensor_tensor(
                    bb[:], aa[:], 1.0, st_q[:], ALU.mult, ALU.mult)); d += 1
                vector.wait_ge(sDVE.sem, d)
                sDVE.inc(vector.tensor_scalar(bb[:], bb[:], -1.0, None,
                                              ALU.mult)); d += 1
                vector.wait_ge(sDVE.sem, d)
                sDVE.inc(vector.tensor_copy(aa[:], st_q[:]))

            @blk.scalar
            def _(scalar):
                scalar.wait_ge(sDVE.sem, dveS0 + 4)
                sACT.inc(scalar.activation(st_q[:], st_q[:], AF.Ln,
                                           bias=c_eps[:, 0:1], scale=1.0))
                scalar.wait_ge(sACT.sem, actS0 + 1)
                sACT.inc(scalar.activation(st_q[:], st_q[:], AF.Exp, bias=0.0,
                                           scale=c_mhalf[:, 0:1]))
        assert sDVE.n == dveS0 + 7 and sACT.n == actS0 + 2

        if parts == "A":
            return
        do_pe = parts in ("B1", "AB")
        do_evac = parts == "AB"
        lx1 = [lnx[0].n, lnx[1].n]
        dve1, act1, pe1 = sDVE.n, sACT.n, sPE.n
        with nc.Block() as blk:
            @blk.tensor
            def _(tensor):
                if not do_pe:
                    return
                for tc in range(NTC):
                    tensor.wait_ge(sDVE.sem, dve1 + tc * 9 + 1)
                    if tc >= 2 and do_evac:
                        tensor.wait_ge(sDVE.sem, dve1 + (tc - 2) * 9 + 9)
                        tensor.wait_ge(sACT.sem, act1 + (tc - 1) * 8)
                    b0 = 0 if tc % 2 == 0 else 4
                    for fc in range(NFC):
                        col = (b0 + fc // 4) * 512 + (fc % 4) * 128
                        mm = tensor.matmul(
                            PS[:, col:col + 128],
                            xnt[tc % 2][:, fc * 128:(fc + 1) * 128],
                            idn[:], start=True, stop=True)
                    sPE.inc(mm)

            @blk.scalar
            def _(scalar):
                if not do_evac:
                    return
                for tc in range(NTC):
                    scalar.wait_ge(sPE.sem, pe1 + tc + 1)
                    b0 = 0 if tc % 2 == 0 else 4
                    for fc in range(8, NFC):
                        col = (b0 + fc // 4) * 512 + (fc % 4) * 128
                        sACT.inc(scalar.activation(
                            xn_ft[:, fc * T + tc * 128: fc * T + tc * 128 + 128],
                            PS[:, col:col + 128], AF.Copy, bias=0.0, scale=1.0))

            @blk.vector
            def _(vector):
                for tc in range(NTC):
                    vector.wait_ge(lnx[tc % 2].sem,
                                   lx1[tc % 2] + 16 * (tc // 2 + 1))
                    if tc >= 2 and do_pe:
                        vector.wait_ge(sPE.sem, pe1 + tc - 1)
                    sDVE.inc(vector.tensor_scalar(
                        xnt[tc % 2][:], xt[tc % 2][:], aa[:, tc:tc + 1],
                        bb[:, tc:tc + 1], ALU.mult, ALU.add))
                    if not do_evac:
                        for fc in range(0, NFC, 2):
                            sDVE.inc(vector.engine_nop())
                        continue
                    vector.wait_ge(sPE.sem, pe1 + tc + 1)
                    b0 = 0 if tc % 2 == 0 else 4
                    for fc in range(0, 8):
                        col = (b0 + fc // 4) * 512 + (fc % 4) * 128
                        sDVE.inc(vector.tensor_copy(
                            xn_ft[:, fc * T + tc * 128: fc * T + tc * 128 + 128],
                            PS[:, col:col + 128]))

            @blk.sync
            def _(sync):
                for tc in range(NTC):
                    if src_ready is not None:
                        for c_, v_ in src_ready(tc):
                            sync.wait_ge(c_.sem, v_)
                    if tc >= 2:
                        sync.wait_ge(sDVE.sem, dve1 + (tc - 2) * 9 + 1)
                    lnx[tc % 2].inc(sync.dma_start(
                        out=xt[tc % 2][:],
                        in_=x_src[tc * 128:(tc + 1) * 128, :]), 16)
        if parts == "AB":
            assert sDVE.n == dve1 + 9 * NTC
            assert sACT.n == act1 + 8 * NTC
            assert sPE.n == pe1 + NTC

    def emit_ffn(x_src, x_dst, xn_ft, fctx, src_ready=None, first=False, entry=None):
        fsb = lambda n, s, d: fctx.enter_context(nc.sbuf_tensor(_nm(n), s, d))
        fuse = fsb("f_fuse", [128, NIC * 512], BF16)
        wg2 = [fsb("f_wg0", [128, FDM], BF16), fsb("f_wg1", [128, FDM], BF16)]
        wu2 = [fsb("f_wu0", [128, FDM], BF16), fsb("f_wu1", [128, FDM], BF16)]
        wd2 = [fsb("f_wd0", [128, ID], BF16), fsb("f_wd1", [128, ID], BF16)]
        gsil = fsb("f_gsil", [128, 512], F32)
        uu = fsb("f_uu", [128, 512], F32)
        yst = fsb("f_yst", [128, 512], F32)
        xnew = [fsb("f_xn%d" % s, [128, FDM], F32) for s in range(4)]
        swg = [dsem("wg0"), dsem("wg1")]
        swu = [dsem("wu0"), dsem("wu1")]
        swd = [dsem("wd0"), dsem("wd1")]
        sxn = [dsem("fxn%d" % s) for s in range(4)]
        sxs = [dsem("fxs%d" % s) for s in range(4)]

        prefetched = False
        for tb in range(4):
            pe0, act0, dve0 = sPE.n, sACT.n, sDVE.n
            wgb = [swg[0].n, swg[1].n]
            wub = [swu[0].n, swu[1].n]
            wdb = [swd[0].n, swd[1].n]
            if prefetched:
                wgb = [v - 16 for v in wgb]
                wub = [v - 16 for v in wub]
            skip_ic = 2 if prefetched else 0
            xnb = [s_.n for s_ in sxn]
            xsb = [s_.n for s_ in sxs]
            pe_gu = lambda ic: pe0 + ic + 1
            pe_dn = lambda hc: pe0 + NIC + 2 * hc + 1
            pe_yt = lambda hc: pe0 + NIC + 2 * hc + 2
            act_ic = lambda ic: act0 + ic + 1
            dve_ic = lambda ic: dve0 + ic + 1
            dve_yst = lambda hc: dve0 + NIC + 2 * hc + 1
            dve_add = lambda hc: dve0 + NIC + 2 * hc + 2

            with nc.Block() as blk:
                @blk.tensor
                def _(tensor):
                    for ic in range(NIC):
                        tensor.wait_ge(swg[ic % 2].sem,
                                       wgb[ic % 2] + 16 * (ic // 2 + 1))
                        tensor.wait_ge(swu[ic % 2].sem,
                                       wub[ic % 2] + 16 * (ic // 2 + 1))
                        if ic >= 2:
                            tensor.wait_ge(sACT.sem, act_ic(ic - 2))
                        for f in range(NFC):
                            tensor.matmul(
                                bank(ic % 2),
                                wg2[ic % 2][:, f * 128:(f + 1) * 128],
                                xn_ft[:, f * T + tb * 512: f * T + tb * 512 + 512],
                                start=(f == 0), stop=(f == NFC - 1))
                        for f in range(NFC):
                            mm = tensor.matmul(
                                bank(2 + ic % 2),
                                wu2[ic % 2][:, f * 128:(f + 1) * 128],
                                xn_ft[:, f * T + tb * 512: f * T + tb * 512 + 512],
                                start=(f == 0), stop=(f == NFC - 1))
                        sPE.inc(mm)
                    for hc in range(NFC):
                        tensor.wait_ge(swd[hc % 2].sem,
                                       wdb[hc % 2] + 16 * (hc // 2 + 1))
                        if hc >= 2:
                            tensor.wait_ge(sDVE.sem, dve_yst(hc - 2))
                        for ic in range(NIC):
                            if hc == 0 and ic == 0:
                                tensor.wait_ge(sDVE.sem, dve_ic(NIC - 1))
                            mm = tensor.matmul(
                                bank(4 + hc % 2),
                                wd2[hc % 2][:, ic * 128:(ic + 1) * 128],
                                fuse[:, ic * 512:(ic + 1) * 512],
                                start=(ic == 0), stop=(ic == NIC - 1))
                        sPE.inc(mm)
                        tensor.wait_ge(sDVE.sem, dve_yst(hc))
                        if hc >= 2:
                            tensor.wait_ge(sDVE.sem, dve_add(hc - 2))
                        for s in range(4):
                            col = (6 + hc % 2) * 512 + s * 128
                            mm = tensor.matmul(PS[:, col:col + 128],
                                               yst[:, s * 128:(s + 1) * 128],
                                               idnf[:], start=True, stop=True)
                        sPE.inc(mm)

                @blk.scalar
                def _(scalar):
                    for ic in range(NIC):
                        scalar.wait_ge(sPE.sem, pe_gu(ic))
                        if ic >= 1:
                            scalar.wait_ge(sDVE.sem, dve_ic(ic - 1))
                        scalar.activation(gsil[:], bank(ic % 2), AF.Silu,
                                          bias=0.0, scale=c_gs[:, ic:ic + 1])
                        sACT.inc(scalar.activation(
                            uu[:], bank(2 + ic % 2), AF.Copy, bias=0.0,
                            scale=c_us[:, ic:ic + 1]))

                @blk.vector
                def _(vector):
                    for ic in range(NIC):
                        vector.wait_ge(sACT.sem, act_ic(ic))
                        sDVE.inc(vector.scalar_tensor_tensor(
                            fuse[:, ic * 512:(ic + 1) * 512], gsil[:], 1.0,
                            uu[:], ALU.mult, ALU.mult))
                    for hc in range(NFC):
                        vector.wait_ge(sPE.sem, pe_dn(hc))
                        sDVE.inc(vector.tensor_scalar(
                            yst[:], bank(4 + hc % 2), c_dsh[:, hc:hc + 1],
                            None, ALU.mult))
                        vector.wait_ge(sPE.sem, pe_yt(hc))
                        if hc == 0:
                            for s in range(4):
                                vector.wait_ge(sxn[s].sem, xnb[s] + 16)
                        for s in range(4):
                            col = (6 + hc % 2) * 512 + s * 128
                            vv = vector.scalar_tensor_tensor(
                                xnew[s][:, hc * 128:(hc + 1) * 128],
                                PS[:, col:col + 128], 1.0,
                                xnew[s][:, hc * 128:(hc + 1) * 128],
                                ALU.mult, ALU.add)
                        sDVE.inc(vv)

                @blk.sync
                def _(sync):
                    if tb == 0 and entry is not None:
                        for c_, v_ in entry:
                            sync.wait_ge(c_.sem, v_)
                    for s in range(4):
                        tc = tb * 4 + s
                        if src_ready is not None:
                            for c_, v_ in src_ready(tc):
                                sync.wait_ge(c_.sem, v_)
                        sync.wait_ge(sxs[s].sem, xsb[s])
                        sxn[s].inc(sync.dma_start(
                            out=xnew[s][:],
                            in_=x_src[tc * 128:(tc + 1) * 128, :]), 16)
                    sync.wait_ge(sDVE.sem, dve_add(NFC - 1))
                    for s in range(4):
                        tc = tb * 4 + s
                        sxs[s].inc(sync.dma_start(
                            out=x_dst[tc * 128:(tc + 1) * 128, :],
                            in_=xnew[s][:]), 16)

                @blk.gpsimd
                def _(gpsimd):
                    if tb == 0 and entry is not None:
                        for c_, v_ in entry:
                            gpsimd.wait_ge(c_.sem, v_)
                    for ic in range(skip_ic, NIC):
                        if first and tb == 0 and ic < 2:
                            gpsimd.wait_ge(dsem("cvt").sem, cvt_total)
                        if ic >= 2:
                            gpsimd.wait_ge(sPE.sem, pe_gu(ic - 2))
                        swg[ic % 2].inc(gpsimd.dma_start(out=wg2[ic % 2][:],
                                                         in_=W8g[ic]), 16)
                        swu[ic % 2].inc(gpsimd.dma_start(out=wu2[ic % 2][:],
                                                         in_=W8u[ic]), 16)
                    for hc in range(NFC):
                        if hc >= 2:
                            gpsimd.wait_ge(sPE.sem, pe_dn(hc - 2))
                        swd[hc % 2].inc(gpsimd.dma_start(out=wd2[hc % 2][:],
                                                         in_=W8d[hc]), 16)
                        if hc == 7 and tb < 3:
                            for icn in range(2):
                                swg[icn].inc(gpsimd.dma_start(
                                    out=wg2[icn][:], in_=W8g[icn]), 16)
                                swu[icn].inc(gpsimd.dma_start(
                                    out=wu2[icn][:], in_=W8u[icn]), 16)

            assert sPE.n == pe0 + NIC + 2 * NFC
            assert sACT.n == act0 + NIC
            assert sDVE.n == dve0 + NIC + 2 * NFC
            prefetched = (tb < 3)
        return sxs

    def emit_qkv(xn_ft, qctx):
        qsb = lambda n, s, d: qctx.enter_context(nc.sbuf_tensor(_nm(n), s, d))
        wq2 = [qsb("q_w0", [128, FDM], BF16), qsb("q_w1", [128, FDM], BF16)]
        qkv2 = [qsb("q_s0", [128, FDM], BF16), qsb("q_s1", [128, FDM], BF16)]
        vstage = [qsb("q_v0", [128, FDM], BF16), qsb("q_v1", [128, FDM], BF16)]
        swq = [dsem("qw0"), dsem("qw1")]
        sqst = [dsem("qst0"), dsem("qst1")]
        svst = [dsem("vst0"), dsem("vst1")]

        order = []
        for g in range(4):
            order += list(range(16 + 4 * g, 16 + 4 * g + 4))
            order += list(range(32 + 4 * g, 32 + 4 * g + 4))
            order += list(range(4 * g, 4 * g + 4))
        pe0, act0, dve0 = sPE.n, sACT.n, sDVE.n
        wqb = [swq[0].n, swq[1].n]
        qstb = [sqst[0].n, sqst[1].n]
        vstb = [svst[0].n, svst[1].n]

        pe_sched = []
        acc = pe0
        for j, qc in enumerate(order):
            acc += 1
            mm_inc = acc
            t_inc = None
            if qc >= 32:
                acc += 1
                t_inc = acc
            pe_sched.append((mm_inc, t_inc))
        act_j = lambda j: act0 + j + 1
        dve_sched = []
        acc = dve0
        for j, qc in enumerate(order):
            if qc >= 32:
                acc += 1
                dve_sched.append(acc)
            else:
                dve_sched.append(None)
        qst_cnt = [0, 0]
        vst_cnt = [0, 0]
        st_at = {}
        for j, qc in enumerate(order):
            p = j % 2
            if qc >= 32:
                vst_cnt[p] += 16
                st_at[j] = (svst[p], vstb[p] + vst_cnt[p])
            else:
                qst_cnt[p] += 16
                st_at[j] = (sqst[p], qstb[p] + qst_cnt[p])

        with nc.Block() as blk:
            @blk.tensor
            def _(tensor):
                for j, qc in enumerate(order):
                    base = 0 if j % 2 == 0 else 2048
                    tensor.wait_ge(swq[j % 2].sem,
                                   wqb[j % 2] + 16 * (j // 2 + 1))
                    if j >= 2:
                        tensor.wait_ge(sACT.sem, act_j(j - 2))
                        if order[j - 2] >= 32:
                            tensor.wait_ge(sDVE.sem, dve_sched[j - 2])
                    for f in range(NFC):
                        for tq in range(4):
                            mm = tensor.matmul(
                                PS[:, base + tq * 512: base + tq * 512 + 512],
                                wq2[j % 2][:, f * 128:(f + 1) * 128],
                                xn_ft[:, f * T + tq * 512: f * T + tq * 512 + 512],
                                start=(f == 0), stop=(f == NFC - 1))
                    sPE.inc(mm)
                    if qc >= 32:
                        tensor.wait_ge(sACT.sem, act_j(j))
                        for tcg in range(16):
                            col = base + (tcg // 4) * 512 + (tcg % 4) * 128
                            mm = tensor.matmul(
                                PS[:, col:col + 128],
                                qkv2[j % 2][:, tcg * 128:(tcg + 1) * 128],
                                idn[:], start=True, stop=True)
                        sPE.inc(mm)

            @blk.scalar
            def _(scalar):
                for j, qc in enumerate(order):
                    base = 0 if j % 2 == 0 else 2048
                    scalar.wait_ge(sPE.sem, pe_sched[j][0])
                    if j >= 2:
                        c_, v_ = st_at[j - 2]
                        scalar.wait_ge(c_.sem, v_)
                    sACT.inc(scalar.activation(
                        qkv2[j % 2][:], PS[:, base:base + 2048], AF.Identity,
                        bias=c_bqkv[:, qc:qc + 1], scale=1.0))

            @blk.vector
            def _(vector):
                for j, qc in enumerate(order):
                    if qc < 32:
                        continue
                    base = 0 if j % 2 == 0 else 2048
                    vector.wait_ge(sPE.sem, pe_sched[j][1])
                    if j >= 2 and order[j - 2] >= 32:
                        c_, v_ = st_at[j - 2]
                        vector.wait_ge(c_.sem, v_)
                    for g in range(4):
                        vv = vector.tensor_copy(
                            vstage[j % 2][:, g * 512:(g + 1) * 512],
                            PS[:, base + g * 512: base + g * 512 + 512])
                    sDVE.inc(vv)

            @blk.sync
            def _(sync):
                for j, qc in enumerate(order):
                    head = qc % 16
                    hg, hr = head // 4, head % 4
                    if qc >= 32:
                        sync.wait_ge(sDVE.sem, dve_sched[j])
                        st_at[j][0].inc(sync.dma_start(
                            out=contribVs[hg][hr * 128:(hr + 1) * 128, :],
                            in_=vstage[j % 2][:]), 16)
                    elif qc >= 16:
                        sync.wait_ge(sACT.sem, act_j(j))
                        st_at[j][0].inc(sync.dma_start(
                            out=contribKs[hg][hr * 128:(hr + 1) * 128, :],
                            in_=qkv2[j % 2][:]), 16)
                    else:
                        sync.wait_ge(sACT.sem, act_j(j))
                        st_at[j][0].inc(sync.dma_start(
                            out=Qst[head * 128:(head + 1) * 128, :],
                            in_=qkv2[j % 2][:]), 16)

            @blk.gpsimd
            def _(gpsimd):
                for j, qc in enumerate(order):
                    if j >= 2:
                        gpsimd.wait_ge(sPE.sem, pe_sched[j - 2][0])
                    src = Wqkv[:, qc * 128:(qc + 1) * 128].rearrange(
                        "(fc p) q -> p fc q", p=128)
                    dst = wq2[j % 2][:].rearrange("p (fc q) -> p fc q", q=128)
                    swq[j % 2].inc(gpsimd.dma_start(out=dst, in_=src), 16)
                    if j % 12 == 7:
                        g = j // 12
                        gpsimd.wait_ge(sqst[0].sem, qstb[0] + 64 * g + 32)
                        gpsimd.wait_ge(sqst[1].sem, qstb[1] + 64 * g + 32)
                        sCC.inc(gpsimd.collective_compute(
                            "AllGather", ALU.bypass,
                            replica_groups=[list(range(8))],
                            ins=[contribKs[g].opt()],
                            outs=[gatheredKs[g].opt()]), 1)
                    if j % 12 == 11:
                        g = j // 12
                        gpsimd.wait_ge(svst[0].sem, vstb[0] + 32 * (g + 1))
                        gpsimd.wait_ge(svst[1].sem, vstb[1] + 32 * (g + 1))
                        sCC.inc(gpsimd.collective_compute(
                            "AllGather", ALU.bypass,
                            replica_groups=[list(range(8))],
                            ins=[contribVs[g].opt()],
                            outs=[gatheredVs[g].opt()]), 1)

        assert sACT.n == act0 + 48
        q_store_at = {order[j]: st_at[j] for j in range(48) if order[j] < 16}
        qkv_store_totals = [(sqst[0], sqst[0].n), (sqst[1], sqst[1].n),
                            (svst[0], svst[0].n), (svst[1], svst[1].n)]
        return q_store_at, qkv_store_totals

    def emit_attn(actx, dmabufs, q_store_at, qkv_store_totals, entry=None):
        asb = lambda n, s, d: actx.enter_context(nc.sbuf_tensor(_nm(n), s, d))
        Ksb, Vsb, Qsb, msk = dmabufs
        Psb = [asb("a_p%d" % q, [128, 4096], BF16) for q in range(4)]
        PTs = [asb("a_pt%d" % i, [128, 512], BF16) for i in range(4)]
        Bsb = asb("a_B", [128, 512], F32)
        Ohd = [asb("a_o0", [128, FDM], BF16), asb("a_o1", [128, FDM], BF16)]
        K2 = asb("a_k2", [128, 4096], BF16)
        Q2 = asb("a_q2", [128, FDM], BF16)
        mxc = asb("a_mx", [128, 8], F32)
        nrm = asb("a_nrm", [128, 16], F32)
        negm = asb("a_negm", [128, 16], F32)
        sgp = asb("a_sgp", [128, 64], F32)
        sg = asb("a_sg", [128, 4], F32)
        rsgT = asb("a_rsgT", [1, 512], F32)
        skk = [dsem("kk0"), dsem("kk1")]
        svv = [dsem("vv0"), dsem("vv1")]
        sqq = [dsem("qq0"), dsem("qq1")]
        soh = [dsem("oh0"), dsem("oh1")]
        smk = dsem("mskS")

        with nc.Block() as blk:
            @blk.sync
            def _(sync):
                if entry is not None:
                    for c_, v_ in entry:
                        sync.wait_ge(c_.sem, v_)
                for s in range(4):
                    src = mask_in[s * 512:(s + 1) * 512, :].rearrange(
                        "(qs p) k -> p qs k", p=128)
                    dst = msk[s][:].rearrange("p (qs k) -> p qs k", k=1024)
                    smk.inc(sync.dma_start(out=dst, in_=src), 16)
        mask_total = smk.n

        kb0 = [skk[0].n, skk[1].n]
        vb0 = [svv[0].n, svv[1].n]
        qb0 = [sqq[0].n, sqq[1].n]
        ob0 = [soh[0].n, soh[1].n]

        HS = []
        pe, act, dve = sPE.n, sACT.n, sDVE.n
        for h in range(NH):
            peS, actS, dveS = {}, {}, {}
            pe += 1; peS["mxw0"] = pe
            pe += 1; peS["mxw1"] = pe
            pe += 1; peS["q2"] = pe
            dve += 1; dveS["k2q2"] = dve
            dve += 1; dveS["mxr0"] = dve
            dve += 1; dveS["mxr1"] = dve
            dve += 1; dveS["mxf"] = dve
            dve += 1; dveS["tprod"] = dve
            act += 2; actS["mest"] = act
            dve += 1; dveS["negm"] = dve
            wave_keys = []
            for s in range(4):
                for q in range(4):
                    for wi in range(s + 1):
                        pe += 1; peS[("S", s, q, wi)] = pe
                        act += 1; actS[("exp", s, q, wi)] = act
                        wave_keys.append((s, q, wi))
                dve += 1; dveS[("sig", s)] = dve
                act += 2; actS[("rsg", s)] = act
                for kc in range(N_KC[s]):
                    pe += 1; peS[("T", s, kc)] = pe
                    if kc % 4 < 3:
                        dve += 1; dveS[("PTe", s, kc)] = dve
                    else:
                        act += 1; actS[("PTe", s, kc)] = act
                pe += 1; peS[("rsgT", s)] = pe
                dve += 1; dveS[("rsgTe", s)] = dve
                pe += 1; peS[("Bmm", s)] = pe
                dve += 1; dveS[("Be", s)] = dve
                pe += 1; peS[("AV", s)] = pe
                dve += 1; dveS[("Oe", s)] = dve
            HS.append((peS, actS, dveS, wave_keys))

        guard = {b: None for b in range(8)}

        def bank_waits(eng, banks):
            done = set()
            for b in banks:
                g = guard[b]
                if g is not None and g not in done:
                    eng.wait_ge(g[0], g[1])
                    done.add(g)
                guard[b] = None

        with nc.Block() as blk:
            @blk.tensor
            def _(tensor):
                for h in range(NH):
                    peS, actS, dveS, wave_keys = HS[h]
                    tensor.wait_ge(sDVE.sem, dveS["k2q2"])
                    for w in range(2):
                        if w == 1:
                            tensor.wait_ge(sDVE.sem, dveS["mxr0"])
                        bank_waits(tensor, [0, 1, 2, 3])
                        for g in range(4):
                            mm = tensor.matmul(bank(g), onesb[:],
                                               K2[:, (w * 4 + g) * 512:
                                                  (w * 4 + g + 1) * 512],
                                               start=True, stop=True)
                        sPE.inc(mm)
                        for g in range(4):
                            guard[g] = (sDVE.sem, dveS["mxr%d" % w])
                    tensor.wait_ge(sDVE.sem, dveS["mxr1"])
                    bank_waits(tensor, [7])
                    for qs in range(16):
                        mm = tensor.matmul(
                            PS[:, 7 * 512 + qs: 7 * 512 + qs + 1],
                            Q2[:, qs * 128:(qs + 1) * 128],
                            onesb[:, 0:1], start=True, stop=True)
                    sPE.inc(mm)
                    guard[7] = (sDVE.sem, dveS["tprod"])
                    widx = 0
                    for s in range(4):
                        for q in range(4):
                            qt = s * 4 + q
                            for wi in range(s + 1):
                                bp = widx % 4
                                tensor.wait_ge(sDVE.sem, dveS["negm"])
                                if h == 0 and widx == 0:
                                    tensor.wait_ge(smk.sem, mask_total)
                                if widx >= 3:
                                    tensor.wait_ge(
                                        sACT.sem,
                                        actS[("exp",) + wave_keys[widx - 3]])
                                bank_waits(tensor, [2 * bp, 2 * bp + 1])
                                for ci in range(2):
                                    c = wi * 2 + ci
                                    if c <= s:
                                        kap = Ksb[h % 2][:, c * 512:
                                                         (c + 1) * 512]
                                    else:
                                        cb = c - s - 1
                                        kap = Ksb[h % 2][:, 2048 + cb * 512:
                                                         2048 + cb * 512 + 512]
                                    has_mask = (c == s or c == 2 * s + 1)
                                    mm = tensor.matmul(
                                        PS[:, (2 * bp + ci) * 512:
                                           (2 * bp + ci) * 512 + 512],
                                        Qsb[h % 2][:, qt * 128:(qt + 1) * 128],
                                        kap, start=True, stop=not has_mask)
                                    if has_mask:
                                        moff = 0 if c == s else 512
                                        mm = tensor.matmul(
                                            PS[:, (2 * bp + ci) * 512:
                                               (2 * bp + ci) * 512 + 512],
                                            idn[:],
                                            msk[s][:, q * 1024 + moff:
                                                   q * 1024 + moff + 512],
                                            start=False, stop=True)
                                sPE.inc(mm)
                                gv = (sACT.sem, actS[("exp", s, q, wi)])
                                guard[2 * bp] = gv
                                guard[2 * bp + 1] = gv
                                widx += 1
                        nkc = N_KC[s]
                        na = 4 * (s + 1)

                        def emit_av(kc, nkc=nkc, na=na, s=s, h=h,
                                    dveS=dveS, actS=actS):
                            who = ("PTe", s, kc)
                            if kc % 4 < 3:
                                tensor.wait_ge(sDVE.sem, dveS[who])
                            else:
                                tensor.wait_ge(sACT.sem, actS[who])
                            if kc == 0:
                                bank_waits(tensor, [6])
                            if kc < na:
                                vap = Vsb[h % 2][:, kc * 128:(kc + 1) * 128]
                            else:
                                vap = Vsb[h % 2][:, 2048 + (kc - na) * 128:
                                                 2048 + (kc - na) * 128 + 128]
                            return tensor.matmul(bank(6), vap,
                                                 PTs[kc % 4][:],
                                                 start=(kc == 0),
                                                 stop=(kc == nkc - 1))

                        for kc in range(nkc):
                            if kc == 0:
                                tensor.wait_ge(sACT.sem,
                                               actS[("exp", s, 3, s)])
                            bnk = kc % 4
                            bank_waits(tensor, [bnk])
                            for q in range(4):
                                col = bnk * 512 + q * 128
                                mm = tensor.matmul(
                                    PS[:, col:col + 128],
                                    Psb[q][:, kc * 128:(kc + 1) * 128],
                                    idn[:], start=True, stop=True)
                            sPE.inc(mm)
                            who = ("PTe", s, kc)
                            guard[bnk] = ((sDVE.sem, dveS[who])
                                          if kc % 4 < 3
                                          else (sACT.sem, actS[who]))
                            if kc >= 2:
                                mm = emit_av(kc - 2)
                        tensor.wait_ge(sDVE.sem, dveS[("sig", s)])
                        tensor.wait_ge(sACT.sem, actS[("rsg", s)])
                        bank_waits(tensor, [7])
                        for q in range(4):
                            mm = tensor.matmul(
                                PS[0:1, 7 * 512 + q * 128:
                                   7 * 512 + q * 128 + 128],
                                sg[:, q:q + 1], idnf[:],
                                start=True, stop=True)
                        sPE.inc(mm)
                        guard[7] = (sDVE.sem, dveS[("rsgTe", s)])
                        tensor.wait_ge(sDVE.sem, dveS[("rsgTe", s)])
                        bank_waits(tensor, [7])
                        for q in range(4):
                            mm = tensor.matmul(
                                PS[:, 7 * 512 + q * 128:
                                   7 * 512 + (q + 1) * 128],
                                onesf[0:1, :],
                                rsgT[0:1, q * 128:(q + 1) * 128],
                                start=True, stop=True)
                        sPE.inc(mm)
                        guard[7] = (sDVE.sem, dveS[("Be", s)])
                        mm = emit_av(nkc - 2)
                        mm = emit_av(nkc - 1)
                        sPE.inc(mm)
                        guard[6] = (sDVE.sem, dveS[("Oe", s)])

            @blk.scalar
            def _(scalar):
                for h in range(NH):
                    peS, actS, dveS, wave_keys = HS[h]
                    scalar.wait_ge(sDVE.sem, dveS["tprod"])
                    sACT.inc(scalar.activation(nrm[:], nrm[:], AF.Ln,
                                               bias=c_tiny[:, 0:1], scale=1.0))
                    scalar.wait_ge(sACT.sem, actS["mest"] - 1)
                    sACT.inc(scalar.activation(nrm[:], nrm[:], AF.Exp,
                                               bias=c_mln128[:, 0:1],
                                               scale=c_half[:, 0:1]))
                    widx = 0
                    for s in range(4):
                        for q in range(4):
                            qt = s * 4 + q
                            for wi in range(s + 1):
                                bp = widx % 4
                                scalar.wait_ge(sPE.sem, peS[("S", s, q, wi)])
                                if h >= 1 and widx == 0:
                                    scalar.wait_ge(
                                        sPE.sem,
                                        HS[h - 1][0][("T", 3, N_KC[3] - 1)])
                                sACT.inc(scalar.activation(
                                    Psb[q][:, wi * 1024: wi * 1024 + 1024],
                                    PS[:, bp * 1024: bp * 1024 + 1024],
                                    AF.Exp, bias=negm[:, qt:qt + 1],
                                    scale=c_isq[:, 0:1],
                                    accum_out=sgp[:, qt * 4 + wi:
                                                  qt * 4 + wi + 1]))
                                widx += 1
                        scalar.wait_ge(sDVE.sem, dveS[("sig", s)])
                        sACT.inc(scalar.activation(sg[:], sg[:], AF.Ln,
                                                   bias=c_tiny[:, 0:1],
                                                   scale=1.0))
                        scalar.wait_ge(sACT.sem, actS[("rsg", s)] - 1)
                        sACT.inc(scalar.activation(sg[:], sg[:], AF.Exp,
                                                   bias=0.0,
                                                   scale=c_mone[:, 0:1]))
                        for kc in range(N_KC[s]):
                            if kc % 4 < 3:
                                continue
                            scalar.wait_ge(sPE.sem, peS[("T", s, kc)])
                            sACT.inc(scalar.activation(
                                PTs[kc % 4][:],
                                PS[:, (kc % 4) * 512:(kc % 4) * 512 + 512],
                                AF.Copy, bias=0.0, scale=1.0))

            @blk.vector
            def _(vector):
                for h in range(NH):
                    peS, actS, dveS, wave_keys = HS[h]
                    kv_val = 16 * 2 * (h // 2 + 1)
                    vector.wait_ge(skk[h % 2].sem, kb0[h % 2] + kv_val)
                    vector.wait_ge(svv[h % 2].sem, vb0[h % 2] + kv_val)
                    vector.wait_ge(sqq[h % 2].sem,
                                   qb0[h % 2] + 16 * (h // 2 + 1))
                    vector.scalar_tensor_tensor(
                        K2[:], Ksb[h % 2][:], 1.0, Ksb[h % 2][:],
                        ALU.mult, ALU.mult)
                    sDVE.inc(vector.scalar_tensor_tensor(
                        Q2[:], Qsb[h % 2][:], 1.0, Qsb[h % 2][:],
                        ALU.mult, ALU.mult))
                    for w in range(2):
                        vector.wait_ge(sPE.sem, peS["mxw%d" % w])
                        for g in range(4):
                            vv = vector.reduce_max(
                                mxc[:, w * 4 + g:w * 4 + g + 1],
                                bank(g), mybir.AxisListType.X)
                        sDVE.inc(vv)
                    vector.wait_ge(sDVE.sem, dveS["mxr1"])
                    sDVE.inc(vector.reduce_max(mxc[:, 0:1], mxc[:, 0:8],
                                               mybir.AxisListType.X))
                    vector.wait_ge(sDVE.sem, dveS["mxf"])
                    vector.wait_ge(sPE.sem, peS["q2"])
                    sDVE.inc(vector.tensor_scalar(
                        nrm[:], PS[:, 7 * 512: 7 * 512 + 16], mxc[:, 0:1],
                        None, ALU.mult))
                    vector.wait_ge(sACT.sem, actS["mest"])
                    sDVE.inc(vector.tensor_scalar(
                        negm[:], nrm[:], -1.0, None, ALU.mult))
                    for s in range(4):
                        nw = s + 1
                        vector.wait_ge(sACT.sem, actS[("exp", s, 3, nw - 1)])
                        for q in range(4):
                            qt = s * 4 + q
                            if nw == 1:
                                vv = vector.tensor_copy(
                                    sg[:, q:q + 1],
                                    sgp[:, qt * 4:qt * 4 + 1])
                            else:
                                vv = vector.reduce_sum(
                                    sg[:, q:q + 1],
                                    sgp[:, qt * 4:qt * 4 + nw],
                                    mybir.AxisListType.X)
                        sDVE.inc(vv)
                        for kc in range(N_KC[s]):
                            if kc % 4 >= 3:
                                continue
                            vector.wait_ge(sPE.sem, peS[("T", s, kc)])
                            sDVE.inc(vector.tensor_copy(
                                PTs[kc % 4][:],
                                PS[:, (kc % 4) * 512:(kc % 4) * 512 + 512]))
                        vector.wait_ge(sPE.sem, peS[("rsgT", s)])
                        sDVE.inc(vector.tensor_copy(
                            rsgT[0:1, :], PS[0:1, 7 * 512: 7 * 512 + 512]))
                        vector.wait_ge(sPE.sem, peS[("Bmm", s)])
                        sDVE.inc(vector.tensor_copy(
                            Bsb[:], PS[:, 7 * 512: 7 * 512 + 512]))
                        vector.wait_ge(sPE.sem, peS[("AV", s)])
                        vector.wait_ge(sDVE.sem, dveS[("Be", s)])
                        if h >= 2 and s == 0:
                            vector.wait_ge(soh[h % 2].sem,
                                           ob0[h % 2] + 16 * (h // 2))
                        sDVE.inc(vector.scalar_tensor_tensor(
                            Ohd[h % 2][:, s * 512:(s + 1) * 512],
                            bank(6), 1.0, Bsb[:], ALU.mult, ALU.mult))

            @blk.sync
            def _(sync):
                for h in range(NH):
                    peS, actS, dveS, wave_keys = HS[h]
                    if h >= 2:
                        sync.wait_ge(sPE.sem, HS[h - 2][0][("AV", 3)])
                    sync.wait_ge(q_store_at[h][0].sem, q_store_at[h][1])
                    sqq[h % 2].inc(sync.dma_start(
                        out=Qsb[h % 2][:],
                        in_=Qst[h * 128:(h + 1) * 128, :]), 16)
                    sync.wait_ge(sDVE.sem, dveS[("Oe", 3)])
                    soh[h % 2].inc(sync.dma_start(
                        out=Ost[h * 128:(h + 1) * 128, :],
                        in_=Ohd[h % 2][:]), 16)

            @blk.gpsimd
            def _(gpsimd):
                for h in range(NH):
                    hg, hr = h // 4, h % 4
                    if hr == 0:
                        gpsimd.wait_ge(sCC.sem, 2 * hg + 1)
                    if h >= 2:
                        gpsimd.wait_ge(sPE.sem, HS[h - 2][0][("AV", 3)])
                    for rank, idx in [(0, idxa), (1, idxb)]:
                        skk[h % 2].inc(gpsimd.indirect_dma_start(
                            out=Ksb[h % 2][:, rank * 2048: rank * 2048 + 2048],
                            out_offset=None, in_=gatheredKs[hg],
                            in_offset=IndirectOffsetOnAxis(ap=idx[:, 0:1],
                                                           axis=0),
                            element_offset=(hr * 128) * T), 16)
                    if hr == 0:
                        gpsimd.wait_ge(sCC.sem, 2 * hg + 2)
                    for rank, idx in [(0, idxa), (1, idxb)]:
                        svv[h % 2].inc(gpsimd.indirect_dma_start(
                            out=Vsb[h % 2][:, rank * 2048: rank * 2048 + 2048],
                            out_offset=None, in_=gatheredVs[hg],
                            in_offset=IndirectOffsetOnAxis(ap=idx[:, 0:1],
                                                           axis=0),
                            element_offset=(hr * 128) * T), 16)

        assert sPE.n == pe and sACT.n == act and sDVE.n == dve
        oh_totals = [(soh[0], soh[0].n), (soh[1], soh[1].n)]
        return oh_totals

    def emit_outproj(x_src, x_dst, octx, oh_totals, src_ready):
        osb = lambda n, s, d: octx.enter_context(nc.sbuf_tensor(_nm(n), s, d))
        wo2 = [osb("o_w0", [128, FDM], BF16), osb("o_w1", [128, FDM], BF16)]
        Ot = [osb("o_t0", [128, 16 * 512], BF16),
              osb("o_t1", [128, 16 * 512], BF16)]
        yst = osb("o_yst", [128, 512], F32)
        xnew = [osb("o_xn%d" % s, [128, FDM], F32) for s in range(4)]
        swo = [dsem("ow0"), dsem("ow1")]
        sot = [dsem("oto0"), dsem("oto1")]
        sxn = [dsem("fxn%d" % s) for s in range(4)]
        sxs = [dsem("fxs%d" % s) for s in range(4)]

        for tb in range(4):
            pe0, act0, dve0 = sPE.n, sACT.n, sDVE.n
            wob = [swo[0].n, swo[1].n]
            otb = [sot[0].n, sot[1].n]
            xnb = [s_.n for s_ in sxn]
            xsb = [s_.n for s_ in sxs]
            pe_mm = lambda hc: pe0 + 2 * hc + 1
            pe_yt = lambda hc: pe0 + 2 * hc + 2
            dve_y = lambda hc: dve0 + 2 * hc + 1
            dve_a = lambda hc: dve0 + 2 * hc + 2

            with nc.Block() as blk:
                @blk.tensor
                def _(tensor):
                    for hc in range(NFC):
                        tensor.wait_ge(swo[hc % 2].sem,
                                       wob[hc % 2] + 16 * (hc // 2 + 1))
                        if hc == 0:
                            tensor.wait_ge(sot[tb % 2].sem, otb[tb % 2] + 16)
                        if hc >= 2:
                            tensor.wait_ge(sDVE.sem, dve_y(hc - 2))
                        for oc in range(NFC):
                            mm = tensor.matmul(
                                bank(hc % 2),
                                wo2[hc % 2][:, oc * 128:(oc + 1) * 128],
                                Ot[tb % 2][:, oc * 512:(oc + 1) * 512],
                                start=(oc == 0), stop=(oc == NFC - 1))
                        sPE.inc(mm)
                        tensor.wait_ge(sDVE.sem, dve_y(hc))
                        if hc >= 2:
                            tensor.wait_ge(sDVE.sem, dve_a(hc - 2))
                        for s in range(4):
                            col = (2 + hc % 2) * 512 + s * 128
                            mm = tensor.matmul(PS[:, col:col + 128],
                                               yst[:, s * 128:(s + 1) * 128],
                                               idnf[:], start=True, stop=True)
                        sPE.inc(mm)

                @blk.vector
                def _(vector):
                    for hc in range(NFC):
                        vector.wait_ge(sPE.sem, pe_mm(hc))
                        if hc >= 1:
                            vector.wait_ge(sPE.sem, pe_yt(hc - 1))
                        sDVE.inc(vector.tensor_scalar(
                            yst[:], bank(hc % 2), c_bo[:, hc:hc + 1],
                            None, ALU.add))
                        vector.wait_ge(sPE.sem, pe_yt(hc))
                        if hc == 0:
                            for s in range(4):
                                vector.wait_ge(sxn[s].sem, xnb[s] + 16)
                        for s in range(4):
                            col = (2 + hc % 2) * 512 + s * 128
                            vv = vector.scalar_tensor_tensor(
                                xnew[s][:, hc * 128:(hc + 1) * 128],
                                PS[:, col:col + 128], 1.0,
                                xnew[s][:, hc * 128:(hc + 1) * 128],
                                ALU.mult, ALU.add)
                        sDVE.inc(vv)

                @blk.sync
                def _(sync):
                    if tb < 2:
                        for c_, v_ in oh_totals:
                            sync.wait_ge(c_.sem, v_)
                    sync.wait_ge(sot[tb % 2].sem, otb[tb % 2])
                    sot[tb % 2].inc(sync.dma_start(
                        out=Ot[tb % 2][:],
                        in_=Ost[:, tb * 512:(tb + 1) * 512].rearrange(
                            "(oc p) t -> p oc t", p=128)), 16)
                    for s in range(4):
                        tc = tb * 4 + s
                        for c_, v_ in src_ready(tc):
                            sync.wait_ge(c_.sem, v_)
                        sync.wait_ge(sxs[s].sem, xsb[s])
                        sxn[s].inc(sync.dma_start(
                            out=xnew[s][:],
                            in_=x_src[tc * 128:(tc + 1) * 128, :]), 16)
                    sync.wait_ge(sDVE.sem, dve_a(NFC - 1))
                    for s in range(4):
                        tc = tb * 4 + s
                        sxs[s].inc(sync.dma_start(
                            out=x_dst[tc * 128:(tc + 1) * 128, :],
                            in_=xnew[s][:]), 16)

                @blk.gpsimd
                def _(gpsimd):
                    for hc in range(NFC):
                        if tb == 0 and hc == 0:
                            gpsimd.wait_ge(dsem("woc").sem, dsem("woc").n)
                        if hc >= 2:
                            gpsimd.wait_ge(sPE.sem, pe_mm(hc - 2))
                        src = WoB[:, hc * 128:(hc + 1) * 128].rearrange(
                            "(oc p) h -> p oc h", p=128)
                        dst = wo2[hc % 2][:].rearrange("p (oc h) -> p oc h",
                                                       h=128)
                        swo[hc % 2].inc(gpsimd.dma_start(out=dst, in_=src), 16)

            assert sPE.n == pe0 + 2 * NFC
            assert sDVE.n == dve0 + 2 * NFC
        return sxs

    def store_ready(sxs_list, base_counts):
        def f(tc):
            s = tc % 4
            k = tc // 4
            return [(sxs_list[s], base_counts[s] + 16 * (k + 1))]
        return f

    def sublayer_ffn(x_src, x_dst, src_ready=None, first=False):
        entry = (None if src_ready is None else
                 [src_ready(12 + i)[0] for i in range(4)])
        with ExitStack() as fctx:
            xn_ft = fctx.enter_context(
                nc.sbuf_tensor(_nm("xn_ft"), [128, NFC * T], BF16))
            with ExitStack() as lctx:
                emit_ln(x_src, xn_ft, lctx, src_ready, entry=entry)
            base = [dsem("fxs%d" % s).n for s in range(4)]
            sxs = emit_ffn(x_src, x_dst, xn_ft, fctx, src_ready, first,
                           entry=entry)
        return store_ready(sxs, base)

    def sublayer_attn(x_src, x_dst, src_ready):
        with ExitStack() as octx0:
            ksb_ = [octx0.enter_context(nc.sbuf_tensor(_nm("a_k%d" % i),
                                                       [128, 4096], BF16))
                    for i in range(2)]
            vsb_ = [octx0.enter_context(nc.sbuf_tensor(_nm("a_v%d" % i),
                                                       [128, 4096], BF16))
                    for i in range(2)]
            qsb_ = [octx0.enter_context(nc.sbuf_tensor(_nm("a_q%d" % i),
                                                       [128, FDM], BF16))
                    for i in range(2)]
            msk_ = [octx0.enter_context(nc.sbuf_tensor(_nm("a_m%d" % i),
                                                       [128, 4096], BF16))
                    for i in range(4)]
            entry = (None if src_ready is None else
                     [src_ready(12 + i)[0] for i in range(4)])
            with ExitStack() as qctx:
                xn_ft = qctx.enter_context(
                    nc.sbuf_tensor(_nm("xn_ft"), [128, NFC * T], BF16))
                with ExitStack() as lctx:
                    emit_ln(x_src, xn_ft, lctx, src_ready, entry=entry)
                q_store_at, qkv_totals = emit_qkv(xn_ft, qctx)
            with ExitStack() as actx:
                oh_totals = emit_attn(actx, (ksb_, vsb_, qsb_, msk_),
                                      q_store_at, qkv_totals, entry=entry)
        with ExitStack() as octx:
            base = [dsem("fxs%d" % s).n for s in range(4)]
            sxs = emit_outproj(x_src, x_dst, octx, oh_totals, src_ready)
        return store_ready(sxs, base)

    if cut is not None and cut.startswith("ln"):
        parts = cut[2:] if len(cut) > 2 else "AB"
        with ExitStack() as fctx:
            xn_ft = fctx.enter_context(
                nc.sbuf_tensor(_nm("xn_ft"), [128, NFC * T], BF16))
            with ExitStack() as lctx:
                emit_ln(xin, xn_ft, lctx, None, parts)
    elif stages == 1:
        sublayer_ffn(xin, out, None, first=True)
    elif stages == 2:
        r1 = sublayer_ffn(xin, xa, None, first=True)
        sublayer_attn(xa, out, r1)
    else:
        r1 = sublayer_ffn(xin, xa, None, first=True)
        r2 = sublayer_attn(xa, xb, r1)
        sublayer_ffn(xb, out, r2, first=False)

    with nc.Block() as blk:
        @blk.sync
        def _(sync):
            for nm, c in DS.items():
                sync.wait_ge(c.sem, c.n)

    gctx.close()
    return nc


_CACHE = {}


def _get_nc(stages):
    if stages not in _CACHE:
        _CACHE[stages] = build_program(stages)
    return _CACHE[stages]


def make_core_inputs(inputs):
    x = np.asarray(inputs["L0_attn_x"])
    blocks = {0: A_BLOCKS, 1: B_BLOCKS}

    base = {
        "Wqkv": np.ascontiguousarray(np.asarray(inputs["L0_attn_Wqkv"])),
        "bqkv": np.ascontiguousarray(
            np.asarray(inputs["L0_attn_bqkv"]).reshape(48, 128).T),
        "Wo": np.ascontiguousarray(np.asarray(inputs["L0_attn_Wo"])),
        "bo": np.ascontiguousarray(
            np.asarray(inputs["L0_attn_bo"]).reshape(16, 128).T),
        "Wg": np.ascontiguousarray(np.asarray(inputs["L0_ffn_gate_w"])),
        "Wu": np.ascontiguousarray(np.asarray(inputs["L0_ffn_up_w"])),
        "Wd": np.ascontiguousarray(np.asarray(inputs["L0_ffn_down_w"])),
        "gs": np.ascontiguousarray(
            np.asarray(inputs["L0_ffn_gate_scale"]).reshape(48, 128).T),
        "us": np.ascontiguousarray(
            np.asarray(inputs["L0_ffn_up_scale"]).reshape(48, 128).T),
        "ds": np.ascontiguousarray(
            np.asarray(inputs["L0_ffn_down_scale"]).reshape(16, 128).T),
        "idn": np.eye(128, dtype=ml_dtypes.bfloat16),
        "idnf": np.eye(128, dtype=np.float32),
        "onesb": np.ones((128, 128), dtype=ml_dtypes.bfloat16),
        "onesf": np.ones((128, 128), dtype=np.float32),
    }

    in_maps = []
    for c in range(8):
        b, s = c // 2, c % 2
        blks = blocks[s]
        xs = np.concatenate([x[b, g * 512:(g + 1) * 512, :] for g in blks], 0)
        mask = np.zeros((T, 1024), dtype=np.float32)
        q_glob = np.concatenate(
            [np.arange(g * 512, (g + 1) * 512) for g in blks])
        for s_ in range(4):
            rows = slice(s_ * 512, (s_ + 1) * 512)
            qg = q_glob[rows][:, None]
            ka = np.arange(A_BLOCKS[s_] * 512,
                           (A_BLOCKS[s_] + 1) * 512)[None, :]
            kb = np.arange(B_BLOCKS[s_] * 512,
                           (B_BLOCKS[s_] + 1) * 512)[None, :]
            mask[rows, 0:512] = np.where(ka <= qg, 0.0, MSK_VAL)
            mask[rows, 512:1024] = np.where(kb <= qg, 0.0, MSK_VAL)
        lo = (c // 2) * 2
        m = dict(base)
        m["x"] = np.ascontiguousarray(xs, dtype=np.float32)
        m["mask"] = mask.astype(ml_dtypes.bfloat16)
        m["idxa"] = (lo * 512 + np.arange(128, dtype=np.uint32)
                     ).reshape(128, 1)
        m["idxb"] = ((lo + 1) * 512 + np.arange(128, dtype=np.uint32)
                     ).reshape(128, 1)
        in_maps.append(m)
    return in_maps


def assemble_output(results):
    res = np.zeros((4, 4096, 2048), dtype=np.float32)
    blocks = {0: A_BLOCKS, 1: B_BLOCKS}
    for c in range(8):
        b, s = c // 2, c % 2
        y = np.asarray(results[c]["out"])
        for i, g in enumerate(blocks[s]):
            res[b, g * 512:(g + 1) * 512, :] = y[i * 512:(i + 1) * 512, :]
    return res


def kernel(**inputs):
    nc = _get_nc(3)
    in_maps = make_core_inputs(inputs)
    res = run_bass_kernel_spmd(nc, in_maps, core_ids=list(range(8)))
    return assemble_output(res.results)
